# version 05 (final)
# baseline (speedup 1.0000x reference)
import contextlib
import numpy as np
import ml_dtypes
import concourse.bass as bass
import concourse.mybir as mybir
from concourse.bass_utils import run_bass_kernel_spmd

F32 = mybir.dt.float32
BF16 = mybir.dt.bfloat16
AF = mybir.ActivationFunctionType
ALU = mybir.AluOpType
AX = mybir.AxisListType

NCORES = 8
B, L, D = 16, 2048, 1024
BPC = B // NCORES
T = BPC * L
DI, DS, DCONV, DTR, NL = 2048, 16, 4, 64, 4
IF = 2816
NH, HD = 16, 64
NE, NSH, HM = 8, 2, 512
FC = 512
NFC = T // FC
EPS = 1e-5

CONSTS = {
    "eps": 1e-5,
    "neghalf": -0.5,
    "attn_scale": 0.125,
    "inv6": 1.0 / 6.0,
    "tiny": 1e-12,
}
CONST_IDX = {k: i for i, k in enumerate(CONSTS)}

DRAM_SPEC = {}


def _spec(name, shape, dtype, role):
    DRAM_SPEC[name] = (list(shape), dtype, role)


def define_specs():
    _spec("x0T", [D, T], F32, "in")
    _spec("x0tok", [T, D], F32, "in")
    for i in range(NL):
        _spec(f"inwT{i}", [D, 2 * DI], F32, "in")
        _spec(f"convw{i}", [DI, DCONV], F32, "in")
        _spec(f"convb{i}", [DI], F32, "in")
        _spec(f"xpwT{i}", [DI, DTR + 2 * DS], F32, "in")
        _spec(f"dtpwT{i}", [DTR, DI], F32, "in")
        _spec(f"dtpb{i}", [DI], F32, "in")
        _spec(f"Alog{i}", [DI, DS], F32, "in")
        _spec(f"Dp{i}", [DI], F32, "in")
        _spec(f"outwT{i}", [DI, D], F32, "in")
        _spec(f"nw{i}", [D], F32, "in")
        _spec(f"nb{i}", [D], F32, "in")
    _spec("fnw", [D], F32, "in")
    _spec("fnb", [D], F32, "in")
    _spec("wgate", [IF, D], F32, "in")
    _spec("wup", [IF, D], F32, "in")
    _spec("wdown", [D, IF], F32, "in")
    _spec("WqT", [D, D], F32, "in")
    _spec("WkT", [D, D], F32, "in")
    _spec("WvT", [D, D], F32, "in")
    _spec("WfcT", [D, D], F32, "in")
    _spec("routerw", [D, NE], F32, "in")
    _spec("routerb", [NE], F32, "in")
    _spec("rw1", [NE, D, HM], F32, "in")
    _spec("rw3", [NE, D, HM], F32, "in")
    _spec("rw2", [NE, HM, D], F32, "in")
    _spec("sw1", [NSH, D, HM], F32, "in")
    _spec("sw3", [NSH, D, HM], F32, "in")
    _spec("sw2", [NSH, HM, D], F32, "in")
    _spec("cpool", [128, len(CONSTS)], F32, "in")
    _spec("iden", [128, 128], F32, "in")
    _spec("ones1x128", [1, 128], F32, "in")
    _spec("onescol", [128, 1], F32, "in")
    _spec("Esel", [32, 32 * 128], F32, "in")
    _spec("Wsc", [128, 32 * 64], F32, "in")
    _spec("Wdn", [128, 16], F32, "in")
    _spec("xnT", [D, T], F32, "interm")
    _spec("xn_bf", [D, T], BF16, "interm")
    _spec("resT", [D, T], F32, "interm")
    _spec("hsnbf", [D, T], BF16, "interm")
    _spec("xpreT", [DI, T], F32, "interm")
    _spec("szT", [DI, T], F32, "interm")
    _spec("xcT", [DI, T], F32, "interm")
    _spec("xc_bf", [DI, T], BF16, "interm")
    _spec("dtxbf", [DTR, T], BF16, "interm")
    _spec("BCrows", [2 * DS, T], F32, "interm")
    _spec("dtT", [DI, T], F32, "interm")
    _spec("uT", [DI, T], F32, "interm")
    _spec("myT", [DI, T], BF16, "interm")
    _spec("moutT", [D, T], F32, "interm")
    _spec("sT", [D, T], F32, "interm")
    _spec("g0T", [D, T], F32, "interm")
    _spec("xqtok", [T, D], BF16, "interm")
    _spec("xqT", [D, T], BF16, "interm")
    _spec("wgq", [IF, D], BF16, "interm")
    _spec("wuq", [IF, D], BF16, "interm")
    _spec("wdq", [D, IF], BF16, "interm")
    _spec("wgqT", [D, IF], BF16, "interm")
    _spec("wuqT", [D, IF], BF16, "interm")
    _spec("wdqT", [IF, D], BF16, "interm")
    _spec("hqtok", [T, IF], BF16, "interm")
    _spec("hdbg", [T, IF], F32, "interm")
    _spec("hqT", [IF, T], BF16, "interm")
    _spec("x1T", [D, T], F32, "interm")
    _spec("xn1bf", [D, T], BF16, "interm")
    _spec("xn1T", [D, T], F32, "interm")
    _spec("lgtok", [T, NE], F32, "interm")
    _spec("QT", [D, T], BF16, "interm")
    _spec("KTr", [D, T], BF16, "interm")
    _spec("Vtok", [T, D], BF16, "interm")
    _spec("Pf", [256, T], BF16, "interm")
    _spec("rcptokD", [T, 16], F32, "interm")
    _spec("Ptok", [T, 256], BF16, "interm")
    _spec("Ytok", [T, D], BF16, "interm")
    _spec("YT", [D, T], BF16, "interm")
    _spec("s1T", [D, T], F32, "interm")
    _spec("gatesT", [NE, T], F32, "interm")
    _spec("g1T", [D, T], F32, "interm")
    _spec("outT", [D, T], F32, "out")


define_specs()


class Sems:
    def __init__(self, nc, es):
        self.nc = nc
        self.es = es
        self.h = {}
        self.c = {}

    def get(self, name):
        if name not in self.h:
            self.h[name] = self.es.enter_context(self.nc.semaphore(name))
            self.c[name] = 0
        return self.h[name]

    def cur(self, name):
        self.get(name)
        return self.c[name]

    def add(self, name, amt):
        self.get(name)
        self.c[name] += amt
        return self.c[name]


class Ctx:
    def __init__(self, nc, es, tensors):
        self.nc = nc
        self.es = es
        self.t = tensors
        self.cpool = None

    def ap(self, name):
        return self.t[name]

    def cp(self, name):
        i = CONST_IDX[name]
        return self.cpool[:, i:i + 1]

    def cp1(self, name):
        i = CONST_IDX[name]
        return self.cpool[0:1, i:i + 1]


ENGS = ("tensor", "vector", "scalar", "gpsimd", "sync")


def barrier(ctx):
    nc = ctx.nc
    s = ctx.sems
    targets = [(s.h[k], s.c[k]) for k in s.h]
    with nc.Block() as blk:
        for en in ENGS:
            def mk(e, _t=tuple(targets)):
                for hh, vv in _t:
                    e.wait_ge(hh, vv)
            getattr(blk, en)(mk)


PHASES = {}
PHASE_IO = {}


def phase(name, ins, outs):
    PHASE_IO[name] = (ins, outs)

    def deco(fn):
        PHASES[name] = fn
        return fn
    return deco


def frearr(ap_, p=128):
    return ap_.rearrange("(k p) t -> p k t", p=p)


def emit_ln_feature(ctx, src, dsts, gamma=None, beta=None, add_src=None,
                    sum_dst=None, tag="ln"):
    nc = ctx.nc
    s = ctx.sems
    KT_ = D // 128
    add = add_src is not None
    ld0, st0, tc0, vc0, sc0 = (s.cur(x) for x in ("ld", "st", "tc", "vc", "sc"))
    pvc = 3 + (1 if add else 0)
    psc = 2
    ptc = 2
    nld_pre = 2 + (2 * KT_ if gamma is not None else 0)
    nld_per = 1 + (1 if add else 0)
    nst_per = len(dsts) + (1 if sum_dst is not None else 0)

    def vA(f):
        return vc0 + pvc * f + (1 if add else 0)

    def vStats(f):
        return vA(f) + 1

    def vAm(f):
        return vA(f) + 2

    def vNorm(f):
        return vc0 + pvc * (f + 1)

    def scSq(f):
        return sc0 + psc * f + 1

    def scR(f):
        return sc0 + psc * (f + 1)

    def tcSum(f):
        return tc0 + ptc * f + 1

    def tcB(f):
        return tc0 + ptc * (f + 1)

    def ldF(f):
        return ld0 + 16 * (nld_pre + nld_per * (f + 1))

    def stF(f):
        return st0 + 16 * nst_per * (f + 1)

    with contextlib.ExitStack() as ps:
        xsb = ps.enter_context(nc.sbuf_tensor(f"{tag}_x", [128, KT_ * FC], F32))
        x2sb = ps.enter_context(nc.sbuf_tensor(f"{tag}_x2", [128, KT_ * FC], F32))
        qsb = ps.enter_context(nc.sbuf_tensor(f"{tag}_q", [128, KT_ * FC], F32))
        onec = ps.enter_context(nc.sbuf_tensor(f"{tag}_onec", [128, 1], F32))
        one128 = ps.enter_context(nc.sbuf_tensor(f"{tag}_one128", [1, 128], F32))
        if gamma is not None:
            gsb = ps.enter_context(nc.sbuf_tensor(f"{tag}_g", [128, KT_], F32))
            bsb = ps.enter_context(nc.sbuf_tensor(f"{tag}_bb", [128, KT_], F32))
        stat = ps.enter_context(nc.sbuf_tensor(f"{tag}_stat", [1, 8 * FC], F32))
        pss = ps.enter_context(nc.psum_tensor(f"{tag}_ps", [1, FC], F32))
        psq = ps.enter_context(nc.psum_tensor(f"{tag}_pq", [1, FC], F32))
        psr = ps.enter_context(nc.psum_tensor(f"{tag}_pr", [128, FC], F32))
        psm = ps.enter_context(nc.psum_tensor(f"{tag}_pm", [128, FC], F32))
        m_ = stat[:, 0:FC]
        mq_ = stat[:, FC:2 * FC]
        mm_ = stat[:, 2 * FC:3 * FC]
        var_ = stat[:, 3 * FC:4 * FC]
        ln_ = stat[:, 4 * FC:5 * FC]
        rs_ = stat[:, 5 * FC:6 * FC]
        am_ = stat[:, 6 * FC:7 * FC]
        sqbuf = qsb if add else x2sb

        with nc.Block() as blk:
            @blk.sync
            def _(sy):
                sy.dma_start(onec[:], ctx.ap("onescol")[:]).then_inc(s.h["ld"], 16)
                sy.dma_start(one128[:], ctx.ap("ones1x128")[:]).then_inc(s.h["ld"], 16)
                if gamma is not None:
                    for k in range(KT_):
                        sy.dma_start(gsb[:, k:k + 1], ctx.ap(gamma)[k * 128:(k + 1) * 128]).then_inc(s.h["ld"], 16)
                        sy.dma_start(bsb[:, k:k + 1], ctx.ap(beta)[k * 128:(k + 1) * 128]).then_inc(s.h["ld"], 16)
                for f in range(NFC):
                    if f >= 1:
                        sy.wait_ge(s.h["vc"], vNorm(f - 1))
                        sy.wait_ge(s.h["st"], stF(f - 1))
                    sy.dma_start(xsb[:], frearr(ctx.ap(src))[:, :, f * FC:(f + 1) * FC]).then_inc(s.h["ld"], 16)
                    if add:
                        sy.dma_start(x2sb[:], frearr(ctx.ap(add_src))[:, :, f * FC:(f + 1) * FC]).then_inc(s.h["ld"], 16)
            s.add("ld", 16 * (nld_pre + nld_per * NFC))

            @blk.vector
            def _(v):
                for f in range(NFC):
                    v.wait_ge(s.h["ld"], ldF(f))
                    if add:
                        if f >= 1:
                            v.wait_ge(s.h["st"], stF(f - 1))
                        for k in range(KT_):
                            ins = v.tensor_add(xsb[:, k * FC:(k + 1) * FC], xsb[:, k * FC:(k + 1) * FC],
                                               x2sb[:, k * FC:(k + 1) * FC])
                        ins.then_inc(s.h["vc"], 1)
                    v.wait_ge(s.h["tc"], tcSum(f))
                    v.tensor_scalar_mul(m_, pss[:], 1.0 / D)
                    v.tensor_scalar_mul(mq_, psq[:], 1.0 / D)
                    v.tensor_mul(mm_, m_, m_)
                    v.tensor_sub(var_, mq_, mm_).then_inc(s.h["vc"], 1)
                    v.wait_ge(s.h["sc"], scR(f))
                    v.tensor_mul(am_, m_, rs_).then_inc(s.h["vc"], 1)
                    v.wait_ge(s.h["tc"], tcB(f))
                    if f >= 1:
                        v.wait_ge(s.h["st"], stF(f - 1))
                    for k in range(KT_):
                        xk = xsb[:, k * FC:(k + 1) * FC]
                        qk = qsb[:, k * FC:(k + 1) * FC]
                        v.tensor_mul(qk, xk, psr[:])
                        ins = v.tensor_sub(qk, qk, psm[:])
                        if gamma is not None:
                            ins = v.tensor_scalar(qk, qk, gsb[:, k:k + 1], bsb[:, k:k + 1], ALU.mult, ALU.add)
                    ins.then_inc(s.h["vc"], 1)
            s.add("vc", pvc * NFC)

            @blk.scalar
            def _(sc):
                for f in range(NFC):
                    if add:
                        sc.wait_ge(s.h["vc"], vA(f))
                    else:
                        sc.wait_ge(s.h["ld"], ldF(f))
                    if f >= 1 and add:
                        sc.wait_ge(s.h["st"], stF(f - 1))
                    for k in range(KT_):
                        ins = sc.activation(sqbuf[:, k * FC:(k + 1) * FC], xsb[:, k * FC:(k + 1) * FC], AF.Square)
                    ins.then_inc(s.h["sc"], 1)
                    sc.wait_ge(s.h["vc"], vStats(f))
                    sc.activation(ln_, var_, AF.Ln, bias=ctx.cp1("eps"))
                    sc.activation(rs_, ln_, AF.Exp, scale=ctx.cp1("neghalf")).then_inc(s.h["sc"], 1)
            s.add("sc", psc * NFC)

            @blk.tensor
            def _(tn):
                for f in range(NFC):
                    if f >= 1:
                        tn.wait_ge(s.h["vc"], vNorm(f - 1))
                    tn.wait_ge(s.h["sc"], scSq(f))
                    for k in range(KT_):
                        tn.matmul(pss[:], onec[:], xsb[:, k * FC:(k + 1) * FC], start=(k == 0), stop=(k == KT_ - 1))
                    for k in range(KT_):
                        ins = tn.matmul(psq[:], onec[:], sqbuf[:, k * FC:(k + 1) * FC], start=(k == 0), stop=(k == KT_ - 1))
                    ins.then_inc(s.h["tc"], 1)
                    tn.wait_ge(s.h["vc"], vAm(f))
                    tn.matmul(psr[:], one128[:], rs_, start=True, stop=True)
                    tn.matmul(psm[:], one128[:], am_, start=True, stop=True).then_inc(s.h["tc"], 1)
            s.add("tc", ptc * NFC)

            @blk.gpsimd
            def _(g):
                for f in range(NFC):
                    g.wait_ge(s.h["vc"], vNorm(f))
                    for dname, _bf in dsts:
                        g.dma_start(frearr(ctx.ap(dname))[:, :, f * FC:(f + 1) * FC], qsb[:]).then_inc(s.h["st"], 16)
                    if sum_dst is not None:
                        g.dma_start(frearr(ctx.ap(sum_dst))[:, :, f * FC:(f + 1) * FC], xsb[:]).then_inc(s.h["st"], 16)
                g.wait_ge(s.h["st"], st0 + 16 * nst_per * NFC)
            s.add("st", 16 * nst_per * NFC)
    barrier(ctx)


@phase("ln0", ["x0T", "onescol", "ones1x128"], ["xnT", "xn_bf"])
def p_ln0(ctx):
    emit_ln_feature(ctx, "x0T", [("xnT", False), ("xn_bf", True)], tag="ln0")


def emit_matmul(ctx, tag, wname, K_rows, M_cols, rhs_name, splits, rhs_dt=BF16):
    nc = ctx.nc
    s = ctx.sems
    KT_ = (K_rows + 127) // 128
    MT_ = (M_cols + 127) // 128
    ld0, st0, tc0, sc0, wl0 = (s.cur(x) for x in ("ld", "st", "tc", "sc", "wl"))
    NPS = 4
    with contextlib.ExitStack() as ps:
        wsb = ps.enter_context(nc.sbuf_tensor(f"{tag}_w", [128, KT_ * M_cols], BF16))
        rsb = [ps.enter_context(nc.sbuf_tensor(f"{tag}_r{j}", [128, KT_ * FC], rhs_dt)) for j in range(2)]
        psb = [ps.enter_context(nc.psum_tensor(f"{tag}_p{j}", [128, FC], F32)) for j in range(NPS)]
        osb = [ps.enter_context(nc.sbuf_tensor(f"{tag}_o{j}", [128, FC],
                                               BF16 if any(sp["kind"] == "copy_bf16" for sp in splits) else F32))
               for j in range(NPS)]
        o32 = [ps.enter_context(nc.sbuf_tensor(f"{tag}_q{j}", [128, FC], F32)) for j in range(NPS)]
        biases = {}
        for sp in splits:
            if sp.get("bias"):
                biases[sp["bias"]] = None
        bsb = None
        if biases:
            bsb = ps.enter_context(nc.sbuf_tensor(f"{tag}_b", [128, MT_], F32))

        def ktile_rows(k):
            return min(128, K_rows - k * 128)

        mt_total = NFC * MT_

        with nc.Block() as blk:
            @blk.gpsimd
            def _(g):
                for k in range(KT_):
                    kr = ktile_rows(k)
                    g.dma_start(wsb[0:kr, k * M_cols:(k + 1) * M_cols],
                                ctx.ap(wname)[k * 128:k * 128 + kr, :]).then_inc(s.h["wl"], 16)
                mt_idx = 0
                for f in range(NFC):
                    for m in range(MT_):
                        g.wait_ge(s.h["sc"], sc0 + mt_idx + 1)
                        mr = min(128, M_cols - m * 128)
                        for sp in splits:
                            a = max(sp["lo"], m * 128)
                            b_ = min(sp["hi"], m * 128 + mr)
                            if a >= b_:
                                continue
                            src = (osb if sp["kind"] == "copy_bf16" else o32)[mt_idx % NPS]
                            drow = sp["dst_lo"] + (a - sp["lo"])
                            dst = ctx.ap(sp["dst"])
                            nrow = dst.shape[0]
                            dview = dst.rearrange("(k p) t -> p k t", p=128) if nrow > 128 else dst[:, :]
                            if nrow > 128:
                                assert drow % 128 == 0 and (b_ - a) == 128
                                g.dma_start(dview[:, drow // 128, f * FC:(f + 1) * FC],
                                            src[:, :]).then_inc(s.h["st"], 16)
                            else:
                                g.dma_start(dst[drow:drow + (b_ - a), f * FC:(f + 1) * FC],
                                            src[a - m * 128:b_ - m * 128, :]).then_inc(s.h["st"], 16)
                        mt_idx += 1
                g.wait_ge(s.h["st"], st0 + 16 * _nstores(splits, MT_) * NFC)
            s.add("wl", 16 * KT_)
            s.add("st", 16 * _nstores(splits, MT_) * NFC)

            assert K_rows <= 128 or K_rows % 128 == 0

            @blk.sync
            def _(sy):
                if bsb is not None:
                    for bi, bn in enumerate(biases):
                        for m in range(MT_):
                            mr = min(128, M_cols - m * 128)
                            sy.dma_start(bsb[0:mr, m:m + 1], ctx.ap(bn)[m * 128:m * 128 + mr]).then_inc(s.h["ld"], 16)
                for f in range(NFC):
                    if f >= 2:
                        sy.wait_ge(s.h["tc"], tc0 + MT_ * (f - 1))
                    src = ctx.ap(rhs_name)
                    if K_rows > 128:
                        sy.dma_start(rsb[f % 2][:],
                                     src.rearrange("(k p) t -> p k t", p=128)[:, :, f * FC:(f + 1) * FC]
                                     ).then_inc(s.h["ld"], 16)
                    else:
                        sy.dma_start(rsb[f % 2][0:K_rows, 0:FC],
                                     src[:, f * FC:(f + 1) * FC]).then_inc(s.h["ld"], 16)
            nbias = (len(biases) * MT_) if bsb is not None else 0
            nld_per = 1
            s.add("ld", 16 * (nbias + nld_per * NFC))

            @blk.tensor
            def _(tn):
                tn.wait_ge(s.h["wl"], wl0 + 16 * KT_)
                mt_idx = 0
                for f in range(NFC):
                    tn.wait_ge(s.h["ld"], ld0 + 16 * (nbias + nld_per * (f + 1)))
                    for m in range(MT_):
                        mr = min(128, M_cols - m * 128)
                        if mt_idx >= NPS:
                            tn.wait_ge(s.h["sc"], sc0 + mt_idx - NPS + 1)
                        pt = psb[mt_idx % NPS]
                        for k in range(KT_):
                            kr = ktile_rows(k)
                            ins = tn.matmul(pt[0:mr, :],
                                            wsb[0:kr, k * M_cols + m * 128:k * M_cols + m * 128 + mr],
                                            rsb[f % 2][0:kr, k * FC:(k + 1) * FC],
                                            start=(k == 0), stop=(k == KT_ - 1))
                        ins.then_inc(s.h["tc"], 1)
                        mt_idx += 1
            s.add("tc", mt_total)

            @blk.scalar
            def _(sc):
                mt_idx = 0
                for f in range(NFC):
                    for m in range(MT_):
                        mr = min(128, M_cols - m * 128)
                        sc.wait_ge(s.h["tc"], tc0 + mt_idx + 1)
                        if mt_idx >= NPS:
                            sc.wait_ge(s.h["st"], st0 + 16 * _stores_before(splits, MT_, mt_idx - NPS + 1))
                        pt = psb[mt_idx % NPS]
                        last = None
                        for sp in splits:
                            a = max(sp["lo"], m * 128)
                            b_ = min(sp["hi"], m * 128 + mr)
                            if a >= b_:
                                continue
                            ra, rb = a - m * 128, b_ - m * 128
                            dstbuf = (osb if sp["kind"] == "copy_bf16" else o32)[mt_idx % NPS]
                            bias_ap = bsb[ra:rb, m:m + 1] if sp.get("bias") else None
                            if sp["kind"] == "silu_f32":
                                last = sc.activation(dstbuf[ra:rb, :], pt[ra:rb, :], AF.Silu)
                            elif bias_ap is not None:
                                last = sc.activation(dstbuf[ra:rb, :], pt[ra:rb, :], AF.Identity, bias=bias_ap)
                            else:
                                last = sc.copy(dstbuf[ra:rb, :], pt[ra:rb, :])
                        last.then_inc(s.h["sc"], 1)
                        mt_idx += 1
            s.add("sc", mt_total)
    barrier(ctx)


def _nstores(splits, MT_):
    n = 0
    for m in range(MT_):
        for sp in splits:
            if max(sp["lo"], m * 128) < min(sp["hi"], (m + 1) * 128):
                n += 1
    return n


def _stores_before(splits, MT_, n_mtiles):
    n = 0
    for mt in range(n_mtiles):
        m = mt % MT_
        for sp in splits:
            if max(sp["lo"], m * 128) < min(sp["hi"], (m + 1) * 128):
                n += 1
    return n


def _rhs_view(ctx, name, buf, f, KT_, K_rows):
    return None


def reg_mamba_phases():
    for i in range(NL):
        def p_inproj(ctx, i=i):
            emit_matmul(ctx, f"ip{i}", f"inwT{i}", D, 2 * DI, "hsnbf",
                        [dict(lo=0, hi=DI, dst="xpreT", dst_lo=0, kind="copy_f32"),
                         dict(lo=DI, hi=2 * DI, dst="szT", dst_lo=0, kind="silu_f32")])
        phase(f"inproj{i}", [f"inwT{i}", "hsnbf"], ["xpreT", "szT"])(p_inproj)

        def p_conv(ctx, i=i):
            emit_conv(ctx, i)
        phase(f"conv{i}", [f"convw{i}", f"convb{i}", "xpreT"], ["xcT", "xc_bf"])(p_conv)

        def p_xproj(ctx, i=i):
            emit_matmul(ctx, f"xp{i}", f"xpwT{i}", DI, DTR + 2 * DS, "xc_bf",
                        [dict(lo=0, hi=DTR, dst="dtxbf", dst_lo=0, kind="copy_bf16"),
                         dict(lo=DTR, hi=DTR + 2 * DS, dst="BCrows", dst_lo=0, kind="copy_f32")])
        phase(f"xproj{i}", [f"xpwT{i}", "xc_bf"], ["dtxbf", "BCrows"])(p_xproj)

        def p_dtproj(ctx, i=i):
            emit_dtproj(ctx, i)
        phase(f"dtproj{i}", [f"dtpwT{i}", f"dtpb{i}", "dtxbf", "xcT"], ["dtT", "uT"])(p_dtproj)

        def p_scan(ctx, i=i):
            emit_scan(ctx, i)
        phase(f"scan{i}", [f"Alog{i}", f"Dp{i}", "dtT", "uT", "xcT", "szT", "BCrows", "Esel"], ["myT"])(p_scan)

        def p_outproj(ctx, i=i):
            emit_matmul(ctx, f"op{i}", f"outwT{i}", DI, D, "myT",
                        [dict(lo=0, hi=D, dst="moutT", dst_lo=0, kind="copy_f32")])
        phase(f"outproj{i}", [f"outwT{i}", "myT"], ["moutT"])(p_outproj)

    def p_lnm0(ctx):
        emit_ln_feature(ctx, "xnT", [("hsnbf", True)], gamma="nw0", beta="nb0", tag="lnm0")
    phase("lnm0", ["xnT", "nw0", "nb0", "onescol", "ones1x128"], ["hsnbf"])(p_lnm0)

    for i in range(1, NL):
        def p_lnm(ctx, i=i):
            emit_ln_feature(ctx, "moutT", [("hsnbf", True)], gamma=f"nw{i}", beta=f"nb{i}",
                            add_src=("xnT" if i == 1 else "resT"), sum_dst="resT", tag=f"lnm{i}")
        phase(f"lnm{i}", ["moutT", f"nw{i}", f"nb{i}", ("xnT" if i == 1 else "resT"),
                          "onescol", "ones1x128"], ["hsnbf", "resT"])(p_lnm)

    def p_lnfin(ctx):
        emit_ln_feature(ctx, "moutT", [("sT", False)], gamma="fnw", beta="fnb",
                        add_src="resT", tag="lnfin")
    phase("lnfin", ["moutT", "fnw", "fnb", "resT", "onescol", "ones1x128"], ["sT"])(p_lnfin)


def emit_conv(ctx, i):
    nc = ctx.nc
    s = ctx.sems
    DT_ = DI // 128
    ld0, st0, vc0, sc0 = (s.cur(x) for x in ("ld", "st", "vc", "sc"))
    with contextlib.ExitStack() as ps:
        xt = [ps.enter_context(nc.sbuf_tensor(f"cv{i}_x{j}", [128, DCONV - 1 + L], F32)) for j in range(2)]
        acc = [ps.enter_context(nc.sbuf_tensor(f"cv{i}_a{j}", [128, L], F32)) for j in range(2)]
        oc = [ps.enter_context(nc.sbuf_tensor(f"cv{i}_o{j}", [128, L], F32)) for j in range(2)]
        ob = [ps.enter_context(nc.sbuf_tensor(f"cv{i}_ob{j}", [128, L], BF16) ) for j in range(2)]
        cw = ps.enter_context(nc.sbuf_tensor(f"cv{i}_w", [128, DT_ * DCONV], F32))
        cb = ps.enter_context(nc.sbuf_tensor(f"cv{i}_c", [128, DT_], F32))
        tiles = [(b, d) for b in range(BPC) for d in range(DT_)]
        with nc.Block() as blk:
            @blk.sync
            def _(sy):
                sy.dma_start(cw[:], ctx.ap(f"convw{i}").rearrange("(k p) c -> p k c", p=128)).then_inc(s.h["ld"], 16)
                for d in range(DT_):
                    sy.dma_start(cb[:, d:d + 1], ctx.ap(f"convb{i}")[d * 128:(d + 1) * 128]).then_inc(s.h["ld"], 16)
                for j, (b, d) in enumerate(tiles):
                    if j >= 2:
                        sy.wait_ge(s.h["vc"], vc0 + (j - 1))
                    sy.dma_start(xt[j % 2][:, DCONV - 1:], frearr(ctx.ap("xpreT"))[:, d, b * L:(b + 1) * L]).then_inc(s.h["ld"], 16)
            s.add("ld", 16 * (1 + DT_ + len(tiles)))

            @blk.gpsimd
            def _(g):
                g.memset(xt[0][:, 0:DCONV - 1], 0.0)
                g.memset(xt[1][:, 0:DCONV - 1], 0.0).then_inc(s.h["st"], 16)
                for j, (b, d) in enumerate(tiles):
                    g.wait_ge(s.h["sc"], sc0 + 2 * (j + 1))
                    g.dma_start(frearr(ctx.ap("xcT"))[:, d, b * L:(b + 1) * L], oc[j % 2][:]).then_inc(s.h["st"], 16)
                    g.dma_start(frearr(ctx.ap("xc_bf"))[:, d, b * L:(b + 1) * L], ob[j % 2][:]).then_inc(s.h["st"], 16)
                g.wait_ge(s.h["st"], st0 + 16 * (1 + 2 * len(tiles)))
            s.add("st", 16 * (1 + 2 * len(tiles)))

            @blk.vector
            def _(v):
                v.wait_ge(s.h["st"], st0 + 16)
                for j, (b, d) in enumerate(tiles):
                    v.wait_ge(s.h["ld"], ld0 + 16 * (1 + DT_ + j + 1))
                    if j >= 2:
                        v.wait_ge(s.h["sc"], sc0 + 2 * (j - 1))
                    x_ = xt[j % 2]
                    a_ = acc[j % 2]
                    v.tensor_scalar_mul(a_[:], x_[:, 0:L], cw[:, d * DCONV:d * DCONV + 1])
                    for q in range(1, DCONV):
                        ins = v.scalar_tensor_tensor(a_[:], x_[:, q:q + L], cw[:, d * DCONV + q:d * DCONV + q + 1],
                                                     a_[:], ALU.mult, ALU.add)
                    ins.then_inc(s.h["vc"], 1)
            s.add("vc", len(tiles))

            @blk.scalar
            def _(sc):
                for j, (b, d) in enumerate(tiles):
                    sc.wait_ge(s.h["vc"], vc0 + j + 1)
                    if j >= 2:
                        sc.wait_ge(s.h["st"], st0 + 16 * (1 + 2 * (j - 1)))
                    sc.activation(oc[j % 2][:], acc[j % 2][:], AF.Silu, bias=cb[:, d:d + 1]).then_inc(s.h["sc"], 1)
                    sc.copy(ob[j % 2][:], oc[j % 2][:]).then_inc(s.h["sc"], 1)
            s.add("sc", 2 * len(tiles))
    barrier(ctx)


def emit_dtproj(ctx, i):
    nc = ctx.nc
    s = ctx.sems
    MT_ = DI // 128
    ld0, st0, tc0, vc0, sc0, wl0 = (s.cur(x) for x in ("ld", "st", "tc", "vc", "sc", "wl"))
    NPS = 4
    with contextlib.ExitStack() as ps:
        wsb = ps.enter_context(nc.sbuf_tensor(f"dp{i}_w", [64, DI], BF16))
        bsb = ps.enter_context(nc.sbuf_tensor(f"dp{i}_b", [128, MT_], F32))
        rsb = [ps.enter_context(nc.sbuf_tensor(f"dp{i}_r{j}", [64, FC], BF16)) for j in range(2)]
        xcb = [ps.enter_context(nc.sbuf_tensor(f"dp{i}_xc{j}", [128, MT_ * FC], F32)) for j in range(2)]
        psb = [ps.enter_context(nc.psum_tensor(f"dp{i}_p{j}", [128, FC], F32)) for j in range(NPS)]
        t1 = ps.enter_context(nc.sbuf_tensor(f"dp{i}_t1", [128, FC], F32))
        t2 = ps.enter_context(nc.sbuf_tensor(f"dp{i}_t2", [128, FC], F32))
        t3 = ps.enter_context(nc.sbuf_tensor(f"dp{i}_t3", [128, FC], F32))
        t4 = ps.enter_context(nc.sbuf_tensor(f"dp{i}_t4", [128, FC], F32))
        dts = [ps.enter_context(nc.sbuf_tensor(f"dp{i}_d{j}", [128, FC], F32)) for j in range(NPS)]
        uts = [ps.enter_context(nc.sbuf_tensor(f"dp{i}_u{j}", [128, FC], F32)) for j in range(NPS)]
        with nc.Block() as blk:
            @blk.gpsimd
            def _(g):
                g.dma_start(wsb[:], ctx.ap(f"dtpwT{i}")[:]).then_inc(s.h["wl"], 16)
                mt_idx = 0
                for f in range(NFC):
                    for m in range(MT_):
                        g.wait_ge(s.h["vc"], vc0 + mt_idx + 1)
                        g.dma_start(frearr(ctx.ap("dtT"))[:, m, f * FC:(f + 1) * FC], dts[mt_idx % NPS][:]).then_inc(s.h["st"], 16)
                        g.dma_start(frearr(ctx.ap("uT"))[:, m, f * FC:(f + 1) * FC], uts[mt_idx % NPS][:]).then_inc(s.h["st"], 16)
                        mt_idx += 1
                g.wait_ge(s.h["st"], st0 + 16 * 2 * NFC * MT_)
            s.add("wl", 16)
            s.add("st", 16 * 2 * NFC * MT_)

            @blk.sync
            def _(sy):
                for m in range(MT_):
                    sy.dma_start(bsb[:, m:m + 1], ctx.ap(f"dtpb{i}")[m * 128:(m + 1) * 128]).then_inc(s.h["ld"], 16)
                for f in range(NFC):
                    if f >= 2:
                        sy.wait_ge(s.h["tc"], tc0 + MT_ * (f - 1))
                        sy.wait_ge(s.h["vc"], vc0 + MT_ * (f - 1))
                    sy.dma_start(rsb[f % 2][:], ctx.ap("dtxbf")[:, f * FC:(f + 1) * FC]).then_inc(s.h["ld"], 16)
                    sy.dma_start(xcb[f % 2][:], frearr(ctx.ap("xcT"))[:, :, f * FC:(f + 1) * FC]).then_inc(s.h["ld"], 16)
            s.add("ld", 16 * (MT_ + 2 * NFC))

            @blk.tensor
            def _(tn):
                tn.wait_ge(s.h["wl"], wl0 + 16)
                mt_idx = 0
                for f in range(NFC):
                    tn.wait_ge(s.h["ld"], ld0 + 16 * (MT_ + 2 * (f + 1)))
                    for m in range(MT_):
                        if mt_idx >= NPS:
                            tn.wait_ge(s.h["sc"], sc0 + 4 * (mt_idx - NPS + 1))
                        tn.matmul(psb[mt_idx % NPS][:], wsb[:, m * 128:(m + 1) * 128], rsb[f % 2][:],
                                  start=True, stop=True).then_inc(s.h["tc"], 1)
                        mt_idx += 1
            s.add("tc", NFC * MT_)

            @blk.scalar
            def _(sc):
                mt_idx = 0
                for f in range(NFC):
                    for m in range(MT_):
                        sc.wait_ge(s.h["tc"], tc0 + mt_idx + 1)
                        if mt_idx >= 1:
                            sc.wait_ge(s.h["vc"], vc0 + mt_idx)
                        pt = psb[mt_idx % NPS]
                        b_ = bsb[:, m:m + 1]
                        sc.activation(t1[:], pt[:], AF.Abs, bias=b_)
                        sc.activation(t2[:], t1[:], AF.Exp, scale=-1.0)
                        sc.activation(t3[:], t2[:], AF.Ln, bias=1.0)
                        sc.activation(t4[:], pt[:], AF.Relu, bias=b_).then_inc(s.h["sc"], 4)
                        mt_idx += 1
            s.add("sc", 4 * NFC * MT_)

            @blk.vector
            def _(v):
                mt_idx = 0
                for f in range(NFC):
                    for m in range(MT_):
                        v.wait_ge(s.h["sc"], sc0 + 4 * (mt_idx + 1))
                        if mt_idx >= NPS:
                            v.wait_ge(s.h["st"], st0 + 16 * 2 * (mt_idx - NPS + 1))
                        db = dts[mt_idx % NPS]
                        ub = uts[mt_idx % NPS]
                        v.tensor_add(db[:], t3[:], t4[:])
                        v.tensor_mul(ub[:], db[:], xcb[f % 2][:, m * FC:(m + 1) * FC]).then_inc(s.h["vc"], 1)
                        mt_idx += 1
            s.add("vc", NFC * MT_)
    barrier(ctx)


def emit_scan(ctx, i):
    nc = ctx.nc
    s = ctx.sems
    DT_ = DI // 128
    NCH = L // FC
    NBC = BPC * NCH
    G = NBC * DT_
    ld0, st0, tc0, vc0, sc0 = (s.cur(x) for x in ("ld", "st", "tc", "vc", "sc"))
    PRE_LD = 2 + DT_
    PER_SC_BC = 2 * DS + DT_ * 2 * DS
    SC_BASE = sc0 + 1
    PVC = 2 * DS + 1

    def gss(g):
        return SC_BASE + (g // DT_) * PER_SC_BC + 2 * DS + (g % DT_) * 2 * DS

    def ldG(g):
        return ld0 + 16 * (PRE_LD + (g // DT_ + 1) + 4 * (g + 1))

    def ldBC(bc):
        return ld0 + 16 * (PRE_LD + (bc + 1) + 4 * (bc * DT_))

    def evac_sc(mm):
        return SC_BASE + (mm // (2 * DS)) * PER_SC_BC + (mm % (2 * DS)) + 1

    def vdone(g):
        return vc0 + PVC * (g + 1)

    with contextlib.ExitStack() as ps:
        esel = ps.enter_context(nc.sbuf_tensor(f"sw{i}_es", [32, 32 * 128], F32))
        Asb = ps.enter_context(nc.sbuf_tensor(f"sw{i}_A", [128, DT_ * DS], F32))
        Atmp = ps.enter_context(nc.sbuf_tensor(f"sw{i}_At", [128, DT_ * DS], F32))
        Dsb = ps.enter_context(nc.sbuf_tensor(f"sw{i}_D", [128, DT_], F32))
        carr = ps.enter_context(nc.sbuf_tensor(f"sw{i}_c", [128, DT_ * DS], F32))
        bcs = [ps.enter_context(nc.sbuf_tensor(f"sw{i}_bc{j}", [32, FC], F32)) for j in range(2)]
        bcrep = ps.enter_context(nc.sbuf_tensor(f"sw{i}_rep", [128, 2 * DS * FC], F32))
        pbc = [ps.enter_context(nc.psum_tensor(f"sw{i}_pb{j}", [128, FC], F32)) for j in range(2)]
        dtb = [ps.enter_context(nc.sbuf_tensor(f"sw{i}_dt{j}", [128, FC], F32)) for j in range(2)]
        ub = [ps.enter_context(nc.sbuf_tensor(f"sw{i}_u{j}", [128, FC], F32)) for j in range(2)]
        xcb = [ps.enter_context(nc.sbuf_tensor(f"sw{i}_xc{j}", [128, FC], F32)) for j in range(2)]
        szb = [ps.enter_context(nc.sbuf_tensor(f"sw{i}_sz{j}", [128, FC], F32)) for j in range(2)]
        asb = ps.enter_context(nc.sbuf_tensor(f"sw{i}_a", [128, FC], F32))
        bsb = ps.enter_context(nc.sbuf_tensor(f"sw{i}_b", [128, FC], F32))
        hsb = ps.enter_context(nc.sbuf_tensor(f"sw{i}_h", [128, FC], F32))
        tmp = ps.enter_context(nc.sbuf_tensor(f"sw{i}_t", [128, FC], F32))
        ysb = ps.enter_context(nc.sbuf_tensor(f"sw{i}_y", [128, FC], F32))
        my = [ps.enter_context(nc.sbuf_tensor(f"sw{i}_my{j}", [128, FC], BF16)) for j in range(2)]

        with nc.Block() as blk:
            @blk.sync
            def _(sy):
                sy.dma_start(esel[:], ctx.ap("Esel")[:]).then_inc(s.h["ld"], 16)
                sy.dma_start(Asb[:].rearrange("p (k n) -> p k n", n=DS),
                             ctx.ap(f"Alog{i}").rearrange("(k p) n -> p k n", p=128)).then_inc(s.h["ld"], 16)
                for d in range(DT_):
                    sy.dma_start(Dsb[:, d:d + 1], ctx.ap(f"Dp{i}")[d * 128:(d + 1) * 128]).then_inc(s.h["ld"], 16)
                g = 0
                for bc in range(NBC):
                    b, c = divmod(bc, NCH)
                    t0_ = b * L + c * FC
                    if bc >= 2:
                        sy.wait_ge(s.h["tc"], tc0 + 2 * DS * (bc - 1))
                    sy.dma_start(bcs[bc % 2][:], ctx.ap("BCrows")[:, t0_:t0_ + FC]).then_inc(s.h["ld"], 16)
                    for d in range(DT_):
                        if g >= 2:
                            sy.wait_ge(s.h["vc"], vdone(g - 2))
                        sy.dma_start(dtb[g % 2][:], frearr(ctx.ap("dtT"))[:, d, t0_:t0_ + FC]).then_inc(s.h["ld"], 16)
                        sy.dma_start(ub[g % 2][:], frearr(ctx.ap("uT"))[:, d, t0_:t0_ + FC]).then_inc(s.h["ld"], 16)
                        sy.dma_start(xcb[g % 2][:], frearr(ctx.ap("xcT"))[:, d, t0_:t0_ + FC]).then_inc(s.h["ld"], 16)
                        sy.dma_start(szb[g % 2][:], frearr(ctx.ap("szT"))[:, d, t0_:t0_ + FC]).then_inc(s.h["ld"], 16)
                        g += 1
            s.add("ld", 16 * (PRE_LD + NBC + 4 * G))

            @blk.tensor
            def _(tn):
                for bc in range(NBC):
                    tn.wait_ge(s.h["ld"], ldBC(bc))
                    for n in range(2 * DS):
                        mm = 2 * DS * bc + n
                        if mm >= 2:
                            tn.wait_ge(s.h["sc"], evac_sc(mm - 2))
                        tn.matmul(pbc[n % 2][:], esel[:, n * 128:(n + 1) * 128], bcs[bc % 2][:],
                                  start=True, stop=True).then_inc(s.h["tc"], 1)
            s.add("tc", 2 * DS * NBC)

            @blk.scalar
            def _(sc):
                sc.wait_ge(s.h["ld"], ld0 + 16 * 2)
                sc.activation(Atmp[:], Asb[:], AF.Exp, scale=1.0)
                sc.mul(Asb[:], Atmp[:], -1.0).then_inc(s.h["sc"], 1)
                for bc in range(NBC):
                    if bc >= 1:
                        sc.wait_ge(s.h["vc"], vc0 + PVC * DT_ * bc)
                    for n in range(2 * DS):
                        sc.wait_ge(s.h["tc"], tc0 + 2 * DS * bc + n + 1)
                        sc.copy(bcrep[:, n * FC:(n + 1) * FC], pbc[n % 2][:]).then_inc(s.h["sc"], 1)
                    for d in range(DT_):
                        g = bc * DT_ + d
                        sc.wait_ge(s.h["ld"], ldG(g))
                        for n in range(DS):
                            if n >= 1:
                                sc.wait_ge(s.h["vc"], vc0 + PVC * g + 2 * n - 1)
                            elif g >= 1:
                                sc.wait_ge(s.h["vc"], vc0 + PVC * g - 2)
                            sc.activation(asb[:], dtb[g % 2][:], AF.Exp,
                                          scale=Asb[:, d * DS + n:d * DS + n + 1]).then_inc(s.h["sc"], 1)
                            sc.wait_ge(s.h["vc"], vc0 + PVC * g + 2 * n + 1)
                            sc.copy(carr[:, d * DS + n:d * DS + n + 1], hsb[:, FC - 1:FC]).then_inc(s.h["sc"], 1)
            s.add("sc", 1 + PER_SC_BC * NBC)

            @blk.vector
            def _(v):
                for g in range(G):
                    bc, d = divmod(g, DT_)
                    b, c = divmod(bc, NCH)
                    v.wait_ge(s.h["ld"], ldG(g))
                    if c == 0:
                        v.memset(carr[:, d * DS:(d + 1) * DS], 0.0)
                    v.tensor_scalar_mul(ysb[:], xcb[g % 2][:], Dsb[:, d:d + 1])
                    for n in range(DS):
                        v.wait_ge(s.h["sc"], gss(g) + 2 * n + 1)
                        v.tensor_mul(bsb[:], ub[g % 2][:], bcrep[:, n * FC:(n + 1) * FC])
                        v.tensor_tensor_scan(hsb[:], asb[:], bsb[:],
                                             carr[:, d * DS + n:d * DS + n + 1],
                                             ALU.mult, ALU.add).then_inc(s.h["vc"], 1)
                        v.tensor_mul(tmp[:], hsb[:], bcrep[:, (DS + n) * FC:(DS + n + 1) * FC])
                        v.tensor_add(ysb[:], ysb[:], tmp[:]).then_inc(s.h["vc"], 1)
                    if g >= 2:
                        v.wait_ge(s.h["st"], st0 + 16 * (g - 1))
                    v.tensor_mul(tmp[:], ysb[:], szb[g % 2][:])
                    v.tensor_copy(my[g % 2][:], tmp[:]).then_inc(s.h["vc"], 1)
            s.add("vc", PVC * G)

            @blk.gpsimd
            def _(g_):
                for g in range(G):
                    bc, d = divmod(g, DT_)
                    b, c = divmod(bc, NCH)
                    g_.wait_ge(s.h["vc"], vdone(g))
                    g_.dma_start(frearr(ctx.ap("myT"))[:, d, b * L + c * FC:b * L + (c + 1) * FC],
                                 my[g % 2][:]).then_inc(s.h["st"], 16)
                g_.wait_ge(s.h["st"], st0 + 16 * G)
            s.add("st", 16 * G)
    barrier(ctx)


reg_mamba_phases()


QMIDS = [0.25, 0.75, 1.25, 1.75, 2.5, 3.5, 5.0]


def vedge(ctx, v, ins):
    qe = ctx.sems.get("qe")
    t = ctx.sems.add("qe", 1)
    ins.then_inc(qe, 1)
    v.wait_ge(qe, t)


def emit_quant_ops(v, x_ap, out_ap, W_, bufs, ctx=None):
    Wb = W_ // 16
    bmax, scale, inv = bufs["bmax"][:, 0:Wb], bufs["scale"][:, 0:Wb], bufs["inv"][:, 0:Wb]
    v_, av, c1, c2, c3 = (bufs[k][:, 0:W_] for k in ("v_", "av", "c1", "c2", "c3"))
    E = lambda ins: vedge(ctx, v, ins)
    E(v.tensor_reduce(bmax, x_ap.rearrange("p (b k) -> p b k", k=16), AX.X, ALU.max,
                      apply_absolute_value=True))
    E(v.tensor_scalar(scale, bmax, 1.0 / 6.0, 1e-12, ALU.mult, ALU.max))
    E(v.reciprocal(inv, scale))
    ibc = bass.AP(bufs["inv"], 0, [[bufs["inv"].shape[1], 128], [1, Wb], [0, 16]])
    E(v.tensor_tensor(v_, x_ap, ibc, ALU.mult))
    E(v.tensor_scalar(av, v_, -1.0, None, ALU.mult))
    E(v.tensor_max(av, av, v_))
    E(v.tensor_scalar(c1, av, QMIDS[0], None, ALU.is_gt))
    for m_ in QMIDS[1:4]:
        E(v.scalar_tensor_tensor(c1, av, m_, c1, ALU.is_gt, ALU.add))
    E(v.tensor_scalar(c2, av, QMIDS[4], None, ALU.is_gt))
    E(v.scalar_tensor_tensor(c2, av, QMIDS[5], c2, ALU.is_gt, ALU.add))
    E(v.tensor_scalar(c3, av, QMIDS[6], 2.0, ALU.is_gt, ALU.mult))
    E(v.scalar_tensor_tensor(c2, c1, 0.5, c2, ALU.mult, ALU.add))
    E(v.tensor_add(c2, c2, c3))
    E(v.tensor_scalar(c1, v_, 0.0, None, ALU.is_ge))
    E(v.tensor_scalar(c1, c1, 2.0, 1.0, ALU.mult, ALU.subtract))
    E(v.tensor_mul(c2, c2, c1))
    sbc = bass.AP(bufs["scale"], 0, [[bufs["scale"].shape[1], 128], [1, Wb], [0, 16]])
    return v.tensor_tensor(out_ap, c2, sbc, ALU.mult)


N_QUANT_OPS = 19


def _mk_quant_bufs(nc, ps, tag, W_):
    return {
        "v_": ps.enter_context(nc.sbuf_tensor(f"{tag}_qv", [128, W_], F32)),
        "av": ps.enter_context(nc.sbuf_tensor(f"{tag}_qa", [128, W_], F32)),
        "c1": ps.enter_context(nc.sbuf_tensor(f"{tag}_q1", [128, W_], F32)),
        "c2": ps.enter_context(nc.sbuf_tensor(f"{tag}_q2", [128, W_], F32)),
        "c3": ps.enter_context(nc.sbuf_tensor(f"{tag}_q3", [128, W_], F32)),
        "bmax": ps.enter_context(nc.sbuf_tensor(f"{tag}_qb", [128, W_ // 16], F32)),
        "scale": ps.enter_context(nc.sbuf_tensor(f"{tag}_qs", [128, W_ // 16], F32)),
        "inv": ps.enter_context(nc.sbuf_tensor(f"{tag}_qi", [128, W_ // 16], F32)),
    }


def emit_quant_phase(ctx, tag, src, dst, R, W_, ln_first=False):
    nc = ctx.nc
    s = ctx.sems
    RT = R // 128
    ld0, st0, vc0, sc0 = (s.cur(x) for x in ("ld", "st", "vc", "sc"))
    pvc = N_QUANT_OPS + (2 if ln_first else 0)
    psc = 2 if ln_first else 0
    with contextlib.ExitStack() as ps:
        xb = [ps.enter_context(nc.sbuf_tensor(f"{tag}_x{j}", [128, W_], F32)) for j in range(2)]
        ob = [ps.enter_context(nc.sbuf_tensor(f"{tag}_o{j}", [128, W_], BF16)) for j in range(2)]
        bufs = _mk_quant_bufs(nc, ps, tag, W_)
        if ln_first:
            stt_ = ps.enter_context(nc.sbuf_tensor(f"{tag}_st", [128, 8], F32))
            xsq = ps.enter_context(nc.sbuf_tensor(f"{tag}_sq", [128, W_], F32))
        with nc.Block() as blk:
            @blk.sync
            def _(sy):
                for r in range(RT):
                    if r >= 2:
                        sy.wait_ge(s.h["vc"], vc0 + pvc * (r - 1))
                    sy.dma_start(xb[r % 2][:], ctx.ap(src)[r * 128:(r + 1) * 128, :]).then_inc(s.h["ld"], 16)
            s.add("ld", 16 * RT)

            if ln_first:
                @blk.scalar
                def _(sc):
                    for r in range(RT):
                        sc.wait_ge(s.h["ld"], ld0 + 16 * (r + 1))
                        if r >= 1:
                            sc.wait_ge(s.h["vc"], vc0 + pvc * r)
                        sc.activation(xsq[:], xb[r % 2][:], AF.Square).then_inc(s.h["sc"], 1)
                        sc.wait_ge(s.h["vc"], vc0 + pvc * r + 1)
                        sc.activation(stt_[:, 4:5], stt_[:, 3:4], AF.Ln, bias=ctx.cp("eps"))
                        sc.activation(stt_[:, 5:6], stt_[:, 4:5], AF.Exp, scale=ctx.cp("neghalf")).then_inc(s.h["sc"], 1)
                s.add("sc", psc * RT)

            @blk.vector
            def _(v):
                for r in range(RT):
                    v.wait_ge(s.h["ld"], ld0 + 16 * (r + 1))
                    x_ = xb[r % 2]
                    if ln_first:
                        v.wait_ge(s.h["sc"], sc0 + psc * r + 1)
                        vedge(ctx, v, v.reduce_sum(stt_[:, 0:1], x_[:], AX.X))
                        vedge(ctx, v, v.tensor_scalar_mul(stt_[:, 0:1], stt_[:, 0:1], 1.0 / W_))
                        vedge(ctx, v, v.reduce_sum(stt_[:, 1:2], xsq[:], AX.X))
                        vedge(ctx, v, v.tensor_scalar_mul(stt_[:, 1:2], stt_[:, 1:2], 1.0 / W_))
                        vedge(ctx, v, v.tensor_mul(stt_[:, 2:3], stt_[:, 0:1], stt_[:, 0:1]))
                        v.tensor_sub(stt_[:, 3:4], stt_[:, 1:2], stt_[:, 2:3]).then_inc(s.h["vc"], 1)
                        v.wait_ge(s.h["sc"], sc0 + psc * (r + 1))
                        v.tensor_scalar(x_[:], x_[:], stt_[:, 0:1], stt_[:, 5:6],
                                        ALU.subtract, ALU.mult).then_inc(s.h["vc"], 1)
                    if r >= 2:
                        v.wait_ge(s.h["st"], st0 + 16 * (r - 1))
                    emit_quant_ops(v, x_[:], ob[r % 2][:], W_, bufs, ctx).then_inc(s.h["vc"], N_QUANT_OPS)
            s.add("vc", pvc * RT)

            @blk.gpsimd
            def _(g):
                for r in range(RT):
                    g.wait_ge(s.h["vc"], vc0 + pvc * (r + 1))
                    g.dma_start(ctx.ap(dst)[r * 128:(r + 1) * 128, :], ob[r % 2][:]).then_inc(s.h["st"], 16)
                g.wait_ge(s.h["st"], st0 + 16 * RT)
            s.add("st", 16 * RT)
    barrier(ctx)


def emit_transpose_phase(ctx, tag, src, dst, R, C):
    nc = ctx.nc
    s = ctx.sems
    RT, CT = R // 128, C // 128
    ld0, st0, tc0, sc0, wl0 = (s.cur(x) for x in ("ld", "st", "tc", "sc", "wl"))
    with contextlib.ExitStack() as ps:
        xb = [ps.enter_context(nc.sbuf_tensor(f"{tag}_x{j}", [128, C], BF16)) for j in range(2)]
        ob = [ps.enter_context(nc.sbuf_tensor(f"{tag}_o{j}", [128, 128], BF16)) for j in range(4)]
        idb = ps.enter_context(nc.sbuf_tensor(f"{tag}_id", [128, 128], BF16))
        pt = [ps.enter_context(nc.psum_tensor(f"{tag}_p{j}", [128, 1024], BF16)) for j in range(2)]
        with nc.Block() as blk:
            @blk.gpsimd
            def _(g):
                g.dma_start(idb[:], ctx.ap("iden")[:]).then_inc(s.h["wl"], 16)
                ti = 0
                for r in range(RT):
                    for c in range(CT):
                        g.wait_ge(s.h["sc"], sc0 + ti + 1)
                        g.dma_start(ctx.ap(dst)[c * 128:(c + 1) * 128, r * 128:(r + 1) * 128],
                                    ob[ti % 4][:]).then_inc(s.h["st"], 16)
                        ti += 1
                g.wait_ge(s.h["st"], st0 + 16 * RT * CT)
            s.add("wl", 16)
            s.add("st", 16 * RT * CT)

            @blk.sync
            def _(sy):
                for r in range(RT):
                    if r >= 2:
                        sy.wait_ge(s.h["tc"], tc0 + CT * (r - 1))
                    sy.dma_start(xb[r % 2][:], ctx.ap(src)[r * 128:(r + 1) * 128, :]).then_inc(s.h["ld"], 16)
            s.add("ld", 16 * RT)

            @blk.tensor
            def _(tn):
                tn.wait_ge(s.h["wl"], wl0 + 16)
                ti = 0
                for r in range(RT):
                    tn.wait_ge(s.h["ld"], ld0 + 16 * (r + 1))
                    for c in range(CT):
                        if ti >= 2:
                            tn.wait_ge(s.h["sc"], sc0 + ti - 1)
                        tn.transpose(pt[ti % 2][:, 0:128], xb[r % 2][:, c * 128:(c + 1) * 128], idb[:]).then_inc(s.h["tc"], 1)
                        ti += 1
            s.add("tc", RT * CT)

            @blk.scalar
            def _(sc):
                ti = 0
                for r in range(RT):
                    for c in range(CT):
                        sc.wait_ge(s.h["tc"], tc0 + ti + 1)
                        if ti >= 4:
                            sc.wait_ge(s.h["st"], st0 + 16 * (ti - 3))
                        sc.copy(ob[ti % 4][:], pt[ti % 2][:, 0:128]).then_inc(s.h["sc"], 1)
                        ti += 1
            s.add("sc", RT * CT)
    barrier(ctx)


IC_CHUNKS = [(0, 512), (512, 512), (1024, 512), (1536, 512), (2048, 512), (2560, 256)]


DBG_H = False


def emit_gateup(ctx):
    nc = ctx.nc
    s = ctx.sems
    KT_ = D // 128
    TT_ = T // 128
    ld0, st0, tc0, vc0, sc0, wl0 = (s.cur(x) for x in ("ld", "st", "tc", "vc", "sc", "wl"))
    NG = len(IC_CHUNKS)
    pvc = 1 + N_QUANT_OPS
    with contextlib.ExitStack() as ps:
        wg = ps.enter_context(nc.sbuf_tensor("gu_wg", [128, KT_ * IF], BF16))
        wu = ps.enter_context(nc.sbuf_tensor("gu_wu", [128, KT_ * IF], BF16))
        lhs = [ps.enter_context(nc.sbuf_tensor(f"gu_l{j}", [128, KT_ * 128], BF16)) for j in range(2)]
        pg = [ps.enter_context(nc.psum_tensor(f"gu_pg{j}", [128, FC], F32)) for j in range(2)]
        pu = [ps.enter_context(nc.psum_tensor(f"gu_pu{j}", [128, FC], F32)) for j in range(2)]
        sg = ps.enter_context(nc.sbuf_tensor("gu_sg", [128, FC], F32))
        su = ps.enter_context(nc.sbuf_tensor("gu_su", [128, FC], F32))
        hh2 = [ps.enter_context(nc.sbuf_tensor(f"gu_h{j}", [128, FC], F32)) for j in range(2)]
        ob = [ps.enter_context(nc.sbuf_tensor(f"gu_o{j}", [128, FC], BF16)) for j in range(2)]
        bufs = _mk_quant_bufs(nc, ps, "gu", FC)
        with nc.Block() as blk:
            @blk.gpsimd
            def _(g):
                for k in range(KT_):
                    g.dma_start(wg[:, k * IF:(k + 1) * IF], ctx.ap("wgqT")[k * 128:(k + 1) * 128, :]).then_inc(s.h["wl"], 16)
                    g.dma_start(wu[:, k * IF:(k + 1) * IF], ctx.ap("wuqT")[k * 128:(k + 1) * 128, :]).then_inc(s.h["wl"], 16)
            s.add("wl", 16 * 2 * KT_)

            @blk.sync
            def _(sy):
                for t in range(TT_):
                    if t >= 2:
                        sy.wait_ge(s.h["tc"], tc0 + NG * (t - 1))
                    sy.dma_start(lhs[t % 2][:].rearrange("p (k c) -> p k c", c=128),
                                 frearr(ctx.ap("xqT"))[:, :, t * 128:(t + 1) * 128]).then_inc(s.h["ld"], 16)
            s.add("ld", 16 * TT_)

            @blk.tensor
            def _(tn):
                tn.wait_ge(s.h["wl"], wl0 + 16 * 2 * KT_)
                gi = 0
                for t in range(TT_):
                    tn.wait_ge(s.h["ld"], ld0 + 16 * (t + 1))
                    for ic, (c0, cw) in enumerate(IC_CHUNKS):
                        if gi >= 2:
                            tn.wait_ge(s.h["sc"], sc0 + gi - 1)
                        for k in range(KT_):
                            tn.matmul(pg[gi % 2][:, 0:cw], lhs[t % 2][:, k * 128:(k + 1) * 128],
                                      wg[:, k * IF + c0:k * IF + c0 + cw], start=(k == 0), stop=(k == KT_ - 1))
                        for k in range(KT_):
                            ins = tn.matmul(pu[gi % 2][:, 0:cw], lhs[t % 2][:, k * 128:(k + 1) * 128],
                                            wu[:, k * IF + c0:k * IF + c0 + cw], start=(k == 0), stop=(k == KT_ - 1))
                        ins.then_inc(s.h["tc"], 1)
                        gi += 1
            s.add("tc", TT_ * NG)

            @blk.scalar
            def _(sc):
                gi = 0
                for t in range(TT_):
                    for ic, (c0, cw) in enumerate(IC_CHUNKS):
                        sc.wait_ge(s.h["tc"], tc0 + gi + 1)
                        if gi >= 1:
                            sc.wait_ge(s.h["vc"], vc0 + pvc * (gi - 1) + 1)
                        sc.activation(sg[:, 0:cw], pg[gi % 2][:, 0:cw], AF.Silu)
                        sc.copy(su[:, 0:cw], pu[gi % 2][:, 0:cw]).then_inc(s.h["sc"], 1)
                        gi += 1
            s.add("sc", TT_ * NG)

            @blk.vector
            def _(v):
                gi = 0
                for t in range(TT_):
                    for ic, (c0, cw) in enumerate(IC_CHUNKS):
                        v.wait_ge(s.h["sc"], sc0 + gi + 1)
                        ins = v.tensor_mul(hh2[gi % 2][:, 0:cw], sg[:, 0:cw], su[:, 0:cw])
                        ins.then_inc(s.h["vc"], 1)
                        v.wait_ge(s.h["vc"], vc0 + pvc * gi + 1)
                        if gi >= 2:
                            v.wait_ge(s.h["st"], st0 + 16 * (gi - 1) * (2 if DBG_H else 1))
                        emit_quant_ops(v, hh2[gi % 2][:, 0:cw], ob[gi % 2][:, 0:cw], cw, bufs, ctx).then_inc(s.h["vc"], N_QUANT_OPS)
                        gi += 1
            s.add("vc", pvc * TT_ * NG)

            @blk.gpsimd
            def _(g):
                gi = 0
                for t in range(TT_):
                    for ic, (c0, cw) in enumerate(IC_CHUNKS):
                        g.wait_ge(s.h["vc"], vc0 + pvc * (gi + 1))
                        g.dma_start(ctx.ap("hqtok")[t * 128:(t + 1) * 128, c0:c0 + cw],
                                    ob[gi % 2][:, 0:cw]).then_inc(s.h["st"], 16)
                        if DBG_H:
                            g.dma_start(ctx.ap("hdbg")[t * 128:(t + 1) * 128, c0:c0 + cw],
                                        hh2[gi % 2][:, 0:cw]).then_inc(s.h["st"], 16)
                        gi += 1
                g.wait_ge(s.h["st"], st0 + 16 * TT_ * NG * (2 if DBG_H else 1))
            s.add("st", 16 * TT_ * NG * (2 if DBG_H else 1))
    barrier(ctx)


def emit_actlhs(ctx, tag, act_name, w_name, F_, dst, out_bf16, fp32=False):
    nc = ctx.nc
    s = ctx.sems
    KT_ = D // 128
    TT_ = T // 128
    FCH = [(i * FC, min(FC, F_ - i * FC)) for i in range((F_ + FC - 1) // FC)]
    ld0, st0, tc0, sc0, wl0 = (s.cur(x) for x in ("ld", "st", "tc", "sc", "wl"))
    NG = len(FCH)
    dt_ = F32 if fp32 else BF16
    NPS = 2
    with contextlib.ExitStack() as ps:
        wsb = ps.enter_context(nc.sbuf_tensor(f"{tag}_w", [128, KT_ * F_], dt_))
        lhs = [ps.enter_context(nc.sbuf_tensor(f"{tag}_l{j}", [128, KT_ * 128], dt_)) for j in range(2)]
        pso = [ps.enter_context(nc.psum_tensor(f"{tag}_p{j}", [128, FC], F32)) for j in range(NPS)]
        ob = [ps.enter_context(nc.sbuf_tensor(f"{tag}_o{j}", [128, min(FC, F_)], BF16 if out_bf16 else F32)) for j in range(NPS)]
        with nc.Block() as blk:
            @blk.gpsimd
            def _(g):
                for k in range(KT_):
                    g.dma_start(wsb[:, k * F_:(k + 1) * F_], ctx.ap(w_name)[k * 128:(k + 1) * 128, :]).then_inc(s.h["wl"], 16)
                gi = 0
                for t in range(TT_):
                    for fi, (c0, cw) in enumerate(FCH):
                        g.wait_ge(s.h["sc"], sc0 + gi + 1)
                        g.dma_start(ctx.ap(dst)[t * 128:(t + 1) * 128, c0:c0 + cw],
                                    ob[gi % NPS][:, 0:cw]).then_inc(s.h["st"], 16)
                        gi += 1
                g.wait_ge(s.h["st"], st0 + 16 * TT_ * NG)
            s.add("wl", 16 * KT_)
            s.add("st", 16 * TT_ * NG)

            @blk.sync
            def _(sy):
                src = ctx.ap(act_name)
                for t in range(TT_):
                    if t >= 2:
                        sy.wait_ge(s.h["tc"], tc0 + NG * (t - 1))
                    sy.dma_start(lhs[t % 2][:].rearrange("p (k c) -> p k c", c=128),
                                 frearr(src)[:, :, t * 128:(t + 1) * 128]).then_inc(s.h["ld"], 16)
            s.add("ld", 16 * TT_)

            @blk.tensor
            def _(tn):
                tn.wait_ge(s.h["wl"], wl0 + 16 * KT_)
                gi = 0
                for t in range(TT_):
                    tn.wait_ge(s.h["ld"], ld0 + 16 * (t + 1))
                    for fi, (c0, cw) in enumerate(FCH):
                        if gi >= NPS:
                            tn.wait_ge(s.h["sc"], sc0 + gi - NPS + 1)
                        for k in range(KT_):
                            ins = tn.matmul(pso[gi % NPS][:, 0:cw], lhs[t % 2][:, k * 128:(k + 1) * 128],
                                            wsb[:, k * F_ + c0:k * F_ + c0 + cw], start=(k == 0), stop=(k == KT_ - 1))
                        ins.then_inc(s.h["tc"], 1)
                        gi += 1
            s.add("tc", TT_ * NG)

            @blk.scalar
            def _(sc):
                gi = 0
                for t in range(TT_):
                    for fi, (c0, cw) in enumerate(FCH):
                        sc.wait_ge(s.h["tc"], tc0 + gi + 1)
                        if gi >= NPS:
                            sc.wait_ge(s.h["st"], st0 + 16 * (gi - NPS + 1))
                        sc.copy(ob[gi % NPS][:, 0:cw], pso[gi % NPS][:, 0:cw]).then_inc(s.h["sc"], 1)
                        gi += 1
            s.add("sc", TT_ * NG)
    barrier(ctx)


def emit_resmul(ctx, tag, a, b_, c_, dst):
    nc = ctx.nc
    s = ctx.sems
    KT_ = D // 128
    W_ = KT_ * FC
    vc0 = s.cur("vc")
    lp0 = [s.cur(f"lp{j}") for j in range(2)]
    sp0 = [s.cur(f"sp{j}") for j in range(2)]
    nf_par = [(NFC + 1) // 2, NFC // 2]
    with contextlib.ExitStack() as ps:
        ab = [ps.enter_context(nc.sbuf_tensor(f"{tag}_a{j}", [128, W_], F32)) for j in range(2)]
        bb = [ps.enter_context(nc.sbuf_tensor(f"{tag}_b{j}", [128, W_], F32)) for j in range(2)]
        cb = [ps.enter_context(nc.sbuf_tensor(f"{tag}_c{j}", [128, W_], F32)) for j in range(2)]
        obuf = [ps.enter_context(nc.sbuf_tensor(f"{tag}_o{j}", [128, W_], F32)) for j in range(2)]
        with nc.Block() as blk:
            @blk.sync
            def _(sy):
                for f in range(NFC):
                    if f >= 2:
                        sy.wait_ge(s.h["vc"], vc0 + 2 * (f - 1))
                    par = f % 2
                    sy.dma_start(ab[par][:], frearr(ctx.ap(a))[:, :, f * FC:(f + 1) * FC]).then_inc(s.h[f"lp{par}"], 16)
                    sy.dma_start(bb[par][:], frearr(ctx.ap(b_))[:, :, f * FC:(f + 1) * FC]).then_inc(s.h[f"lp{par}"], 16)
                    sy.dma_start(cb[par][:], frearr(ctx.ap(c_))[:, :, f * FC:(f + 1) * FC]).then_inc(s.h[f"lp{par}"], 16)
            for j in range(2):
                s.add(f"lp{j}", 16 * 3 * nf_par[j])

            @blk.vector
            def _(v):
                for f in range(NFC):
                    par = f % 2
                    v.wait_ge(s.h[f"lp{par}"], lp0[par] + 16 * 3 * (f // 2 + 1))
                    if f >= 2:
                        v.wait_ge(s.h[f"sp{par}"], sp0[par] + 16 * (f // 2))
                    vedge(ctx, v, v.tensor_mul(obuf[par][:], bb[par][:], cb[par][:]))
                    v.tensor_add(obuf[par][:], obuf[par][:], ab[par][:]).then_inc(s.h["vc"], 2)
            s.add("vc", 2 * NFC)

            @blk.gpsimd
            def _(g):
                for f in range(NFC):
                    par = f % 2
                    g.wait_ge(s.h["vc"], vc0 + 2 * (f + 1))
                    g.dma_start(frearr(ctx.ap(dst))[:, :, f * FC:(f + 1) * FC], obuf[par][:]).then_inc(s.h[f"sp{par}"], 16)
                for j in range(2):
                    g.wait_ge(s.h[f"sp{j}"], sp0[j] + 16 * nf_par[j])
            for j in range(2):
                s.add(f"sp{j}", 16 * nf_par[j])
    barrier(ctx)


def emit_gates(ctx):
    nc = ctx.nc
    s = ctx.sems
    TT_ = T // 128
    PVC = 12
    ld0, st0, tc0, vc0, sc0 = (s.cur(x) for x in ("ld", "st", "tc", "vc", "sc"))
    with contextlib.ExitStack() as ps:
        rb = ps.enter_context(nc.sbuf_tensor("gt_rb", [1, NE], F32))
        one128 = ps.enter_context(nc.sbuf_tensor("gt_o1", [1, 128], F32))
        idf = ps.enter_context(nc.sbuf_tensor("gt_id", [128, 128], F32))
        prb_full = ps.enter_context(nc.psum_tensor("gt_prb", [128, FC], F32))
        prb = prb_full[:, 0:NE]
        lg = [ps.enter_context(nc.sbuf_tensor(f"gt_lg{j}", [128, NE], F32)) for j in range(2)]
        e_ = ps.enter_context(nc.sbuf_tensor("gt_e", [128, NE], F32))
        msk = ps.enter_context(nc.sbuf_tensor("gt_m", [128, NE], F32))
        ke = ps.enter_context(nc.sbuf_tensor("gt_k", [128, NE], F32))
        gts = ps.enter_context(nc.sbuf_tensor("gt_g", [128, NE], F32))
        stt_ = ps.enter_context(nc.sbuf_tensor("gt_s", [128, 8], F32))
        pgt_full = [ps.enter_context(nc.psum_tensor(f"gt_pt{j}", [128, FC], F32)) for j in range(2)]
        pgt = [p[0:NE, 0:128] for p in pgt_full]
        gob = [ps.enter_context(nc.sbuf_tensor(f"gt_go{j}", [NE, 128], F32)) for j in range(2)]
        with nc.Block() as blk:
            @blk.sync
            def _(sy):
                sy.dma_start(rb[:], ctx.ap("routerb")[:]).then_inc(s.h["ld"], 16)
                sy.dma_start(one128[:], ctx.ap("ones1x128")[:]).then_inc(s.h["ld"], 16)
                sy.dma_start(idf[:], ctx.ap("iden")[:]).then_inc(s.h["ld"], 16)
                for t in range(TT_):
                    if t >= 2:
                        sy.wait_ge(s.h["vc"], vc0 + PVC * (t - 1))
                    sy.dma_start(lg[t % 2][:], ctx.ap("lgtok")[t * 128:(t + 1) * 128, :]).then_inc(s.h["ld"], 16)
            s.add("ld", 16 * (3 + TT_))

            @blk.tensor
            def _(tn):
                tn.wait_ge(s.h["ld"], ld0 + 16 * 3)
                tn.matmul(prb, one128[:], rb[:], start=True, stop=True).then_inc(s.h["tc"], 1)
                for t in range(TT_):
                    tn.wait_ge(s.h["vc"], vc0 + PVC * t + 11)
                    if t >= 2:
                        tn.wait_ge(s.h["vc"], vc0 + PVC * (t - 1) + 12)
                    tn.transpose(pgt[t % 2], gts[:, :], idf[:]).then_inc(s.h["tc"], 1)
            s.add("tc", 1 + TT_)

            @blk.vector
            def _(v):
                for t in range(TT_):
                    v.wait_ge(s.h["ld"], ld0 + 16 * (3 + t + 1))
                    v.wait_ge(s.h["tc"], tc0 + 1)
                    vedge(ctx, v, v.tensor_add(lg[t % 2][:], lg[t % 2][:], prb))
                    vedge(ctx, v, v.reduce_max(stt_[:, 0:1], lg[t % 2][:], AX.X))
                    v.reduce_max(stt_[:, 1:2], lg[t % 2][:], AX.X, negate=True).then_inc(s.h["vc"], 3)
                    vedge(ctx, v, v.tensor_scalar(msk[:], lg[t % 2][:], stt_[:, 0:1], -1e30, ALU.is_ge, ALU.mult))
                    vedge(ctx, v, v.tensor_add(msk[:], lg[t % 2][:], msk[:]))
                    v.reduce_max(stt_[:, 2:3], msk[:], AX.X).then_inc(s.h["vc"], 3)
                    v.wait_ge(s.h["sc"], sc0 + t + 1)
                    vedge(ctx, v, v.tensor_scalar(msk[:], lg[t % 2][:], stt_[:, 2:3], None, ALU.is_ge))
                    vedge(ctx, v, v.tensor_mul(ke[:], e_[:], msk[:]))
                    vedge(ctx, v, v.reduce_sum(stt_[:, 3:4], ke[:], AX.X))
                    vedge(ctx, v, v.reciprocal(stt_[:, 4:5], stt_[:, 3:4]))
                    if t >= 1:
                        v.wait_ge(s.h["tc"], tc0 + 1 + t)
                    v.tensor_scalar_mul(gts[:], ke[:], stt_[:, 4:5]).then_inc(s.h["vc"], 5)
                    v.wait_ge(s.h["tc"], tc0 + 1 + t + 1)
                    if t >= 2:
                        v.wait_ge(s.h["st"], st0 + 16 * (t - 1))
                    v.tensor_copy(gob[t % 2][:], pgt[t % 2]).then_inc(s.h["vc"], 1)
            s.add("vc", PVC * TT_)

            @blk.scalar
            def _(sc):
                for t in range(TT_):
                    sc.wait_ge(s.h["vc"], vc0 + PVC * t + 3)
                    if t >= 1:
                        sc.wait_ge(s.h["vc"], vc0 + PVC * (t - 1) + 11)
                    sc.activation(e_[:], lg[t % 2][:], AF.Exp, bias=stt_[:, 1:2]).then_inc(s.h["sc"], 1)
            s.add("sc", TT_)

            @blk.gpsimd
            def _(g):
                for t in range(TT_):
                    g.wait_ge(s.h["vc"], vc0 + PVC * (t + 1))
                    g.dma_start(ctx.ap("gatesT")[:, t * 128:(t + 1) * 128], gob[t % 2][:]).then_inc(s.h["st"], 16)
                g.wait_ge(s.h["st"], st0 + 16 * TT_)
            s.add("st", 16 * TT_)
    barrier(ctx)


def emit_moe(ctx):
    nc = ctx.nc
    s = ctx.sems
    KT_ = D // 128
    HK = HM // 128
    NEX = NE + NSH
    TCF = NEX * HK + 2 * NEX
    SCF = NEX * HK + KT_
    VCF = NEX * HK * 2
    WLF = 2 * NEX + 2 * NEX
    ld0, st0, tc0, vc0, sc0, wl0 = (s.cur(x) for x in ("ld", "st", "tc", "vc", "sc", "wl"))

    def tcA(f, e, hk):
        return tc0 + TCF * f + e * HK + hk + 1

    def tcB(f, half, e):
        return tc0 + TCF * f + NEX * HK + half * NEX + e + 1

    def scA(f, e, hk):
        return sc0 + SCF * f + e * HK + hk + 1

    def scB(f, m):
        return sc0 + SCF * f + NEX * HK + m + 1

    def vhh(f, e, hk):
        return vc0 + VCF * f + 2 * (e * HK + hk) + 1

    def vhq(f, e, hk):
        return vc0 + VCF * f + 2 * (e * HK + hk) + 2

    def wl13(f, e):
        return wl0 + 16 * (WLF * f + 2 * (e + 1))

    def wl2(f, half, e):
        return wl0 + 16 * (WLF * f + 2 * NEX + half * NEX + e + 1)

    with contextlib.ExitStack() as ps:
        x_sb = [ps.enter_context(nc.sbuf_tensor(f"mo_x{j}", [128, KT_ * FC], BF16)) for j in range(2)]
        w13 = [ps.enter_context(nc.sbuf_tensor(f"mo_w13_{j}", [128, 2 * KT_ * HM], BF16)) for j in range(2)]
        w2 = [ps.enter_context(nc.sbuf_tensor(f"mo_w2_{j}", [128, HK * D], BF16)) for j in range(2)]
        gr = ps.enter_context(nc.sbuf_tensor("mo_gr", [1, NE * FC], F32))
        one128 = ps.enter_context(nc.sbuf_tensor("mo_o1", [1, 128], F32))
        h1 = ps.enter_context(nc.sbuf_tensor("mo_h1", [128, NEX * HK * FC], BF16))
        sg = ps.enter_context(nc.sbuf_tensor("mo_sg", [128, FC], F32))
        hh = ps.enter_context(nc.sbuf_tensor("mo_hh", [128, FC], F32))
        g1o = [ps.enter_context(nc.sbuf_tensor(f"mo_g1{j}", [128, FC], F32)) for j in range(4)]
        psA = [ps.enter_context(nc.psum_tensor(f"mo_pa{j}", [128, FC], F32)) for j in range(3)]
        psO = [ps.enter_context(nc.psum_tensor(f"mo_po{j}", [128, FC], F32)) for j in range(4)]
        with nc.Block() as blk:
            @blk.sync
            def _(sy):
                sy.dma_start(one128[:], ctx.ap("ones1x128")[:]).then_inc(s.h["ld"], 16)
                for f in range(NFC):
                    if f >= 1:
                        sy.wait_ge(s.h["tc"], tc0 + TCF * (f - 1) + NEX * HK)
                    if f >= 2:
                        sy.wait_ge(s.h["tc"], tc0 + TCF * (f - 1))
                    sy.dma_start(gr[:].rearrange("o (e c) -> o e c", c=FC),
                                 ctx.ap("gatesT")[0:NE, f * FC:(f + 1) * FC]).then_inc(s.h["ld"], 16)
                    sy.dma_start(x_sb[f % 2][:].rearrange("p (k c) -> p k c", c=FC),
                                 frearr(ctx.ap("xn1bf"))[:, :, f * FC:(f + 1) * FC]).then_inc(s.h["ld"], 16)
            s.add("ld", 16 * (1 + 2 * NFC))

            @blk.gpsimd
            def _(g):
                for f in range(NFC):
                    for e in range(NEX):
                        ge = NEX * f + e
                        if ge >= 2:
                            pe = ge - 2
                            g.wait_ge(s.h["tc"], tcA(pe // NEX, pe % NEX, HK - 1))
                        if e < NE:
                            w1s, w3s = ctx.ap("rw1")[e], ctx.ap("rw3")[e]
                        else:
                            w1s, w3s = ctx.ap("sw1")[e - NE], ctx.ap("sw3")[e - NE]
                        g.dma_start(w13[e % 2][:, 0:KT_ * HM].rearrange("p (k c) -> p k c", c=HM),
                                    w1s.rearrange("(k p) c -> p k c", p=128)).then_inc(s.h["wl"], 16)
                        g.dma_start(w13[e % 2][:, KT_ * HM:].rearrange("p (k c) -> p k c", c=HM),
                                    w3s.rearrange("(k p) c -> p k c", p=128)).then_inc(s.h["wl"], 16)
                    for half in range(2):
                        for e in range(NEX):
                            gb = 2 * NEX * f + NEX * half + e
                            if gb >= 2:
                                pb = gb - 2
                                g.wait_ge(s.h["tc"], tcB(pb // (2 * NEX), (pb % (2 * NEX)) // NEX, pb % NEX))
                            w2s = ctx.ap("rw2")[e] if e < NE else ctx.ap("sw2")[e - NE]
                            g.dma_start(w2[e % 2][:].rearrange("p (k c) -> p k c", c=D),
                                        w2s.rearrange("(k p) c -> p k c", p=128)).then_inc(s.h["wl"], 16)
                    for m in range(KT_):
                        g.wait_ge(s.h["sc"], scB(f, m))
                        g.dma_start(frearr(ctx.ap("g1T"))[:, m, f * FC:(f + 1) * FC],
                                    g1o[m % 4][:]).then_inc(s.h["st"], 16)
                g.wait_ge(s.h["st"], st0 + 16 * NFC * KT_)
            s.add("wl", 16 * NFC * WLF)
            s.add("st", 16 * NFC * KT_)

            @blk.tensor
            def _(tn):
                for f in range(NFC):
                    tn.wait_ge(s.h["ld"], ld0 + 16 * (1 + 2 * (f + 1)))
                    for e in range(NEX):
                        tn.wait_ge(s.h["wl"], wl13(f, e))
                        if e < NE:
                            if e >= 1:
                                tn.wait_ge(s.h["vc"], vhq(f, e - 1, HK - 1))
                            elif f >= 1:
                                tn.wait_ge(s.h["vc"], vc0 + VCF * f)
                            tn.matmul(psA[2][:], one128[:], gr[0:1, e * FC:(e + 1) * FC],
                                      start=True, stop=True)
                        for hk in range(HK):
                            gidx = (NEX * f + e) * HK + hk
                            if gidx >= 1:
                                pf, pr = divmod(gidx - 1, NEX * HK)
                                tn.wait_ge(s.h["sc"], scA(pf, pr // HK, pr % HK))
                                tn.wait_ge(s.h["vc"], vhh(pf, pr // HK, pr % HK))
                            for k in range(KT_):
                                tn.matmul(psA[0][:], w13[e % 2][:, k * HM + hk * 128:k * HM + (hk + 1) * 128],
                                          x_sb[f % 2][:, k * FC:(k + 1) * FC], start=(k == 0), stop=(k == KT_ - 1))
                            for k in range(KT_):
                                ins = tn.matmul(psA[1][:], w13[e % 2][:, KT_ * HM + k * HM + hk * 128:KT_ * HM + k * HM + (hk + 1) * 128],
                                                x_sb[f % 2][:, k * FC:(k + 1) * FC], start=(k == 0), stop=(k == KT_ - 1))
                            ins.then_inc(s.h["tc"], 1)
                    for half in range(2):
                        for e in range(NEX):
                            tn.wait_ge(s.h["wl"], wl2(f, half, e))
                            tn.wait_ge(s.h["vc"], vhq(f, e, HK - 1))
                            if e == 0:
                                if half == 0 and f >= 1:
                                    tn.wait_ge(s.h["sc"], sc0 + SCF * f)
                                if half == 1:
                                    tn.wait_ge(s.h["sc"], scB(f, 3))
                            for m in range(4):
                                mm = half * 4 + m
                                for k in range(HK):
                                    ins = tn.matmul(psO[m][:], w2[e % 2][:, k * D + mm * 128:k * D + (mm + 1) * 128],
                                                    h1[:, (e * HK + k) * FC:(e * HK + k + 1) * FC],
                                                    start=(e == 0 and k == 0), stop=(e == NEX - 1 and k == HK - 1))
                            ins.then_inc(s.h["tc"], 1)
            s.add("tc", NFC * TCF)

            @blk.scalar
            def _(sc):
                for f in range(NFC):
                    for e in range(NEX):
                        for hk in range(HK):
                            gidx = (NEX * f + e) * HK + hk
                            sc.wait_ge(s.h["tc"], tcA(f, e, hk))
                            if gidx >= 1:
                                pf, pr = divmod(gidx - 1, NEX * HK)
                                sc.wait_ge(s.h["vc"], vhh(pf, pr // HK, pr % HK))
                            sc.activation(sg[:], psA[0][:], AF.Silu).then_inc(s.h["sc"], 1)
                    for m in range(KT_):
                        half = m // 4
                        sc.wait_ge(s.h["tc"], tcB(f, half, NEX - 1))
                        if f >= 1:
                            sc.wait_ge(s.h["st"], st0 + 16 * ((f - 1) * KT_ + m + 1))
                        sc.activation(g1o[m % 4][:], psO[m % 4][:], AF.Silu).then_inc(s.h["sc"], 1)
            s.add("sc", NFC * SCF)

            @blk.vector
            def _(v):
                for f in range(NFC):
                    for e in range(NEX):
                        for hk in range(HK):
                            v.wait_ge(s.h["sc"], scA(f, e, hk))
                            if f >= 1 and hk == 0:
                                v.wait_ge(s.h["tc"], tcB(f - 1, 1, e))
                            v.tensor_mul(hh[:], sg[:], psA[1][:]).then_inc(s.h["vc"], 1)
                            dst = h1[:, (e * HK + hk) * FC:(e * HK + hk + 1) * FC]
                            if e < NE:
                                v.tensor_mul(dst, hh[:], psA[2][:]).then_inc(s.h["vc"], 1)
                            else:
                                v.tensor_copy(dst, hh[:]).then_inc(s.h["vc"], 1)
            s.add("vc", NFC * VCF)
    barrier(ctx)


def emit_attn_scores(ctx):
    nc = ctx.nc
    s = ctx.sems
    ld0, st0, tc0, vc0, sc0, wl0 = (s.cur(x) for x in ("ld", "st", "tc", "vc", "sc", "wl"))
    PVC, PTC, PSC, PST = 6, 10, 9, 6

    def tgdone(gg):
        return tc0 + PTC * (gg // 4) + (gg % 4) + 1

    with contextlib.ExitStack() as ps:
        wsc = ps.enter_context(nc.sbuf_tensor("as_w", [128, 32 * 64], BF16))
        wdn = ps.enter_context(nc.sbuf_tensor("as_wd", [128, 16], BF16))
        idf = ps.enter_context(nc.sbuf_tensor("as_id", [128, 128], F32))
        qsb = [ps.enter_context(nc.sbuf_tensor(f"as_q{j}", [128, 8 * FC], BF16)) for j in range(2)]
        k2 = [ps.enter_context(nc.sbuf_tensor(f"as_k{j}", [128, 16 * FC], BF16)) for j in range(2)]
        prod = ps.enter_context(nc.sbuf_tensor("as_pr", [128, 64 * FC], BF16))
        Psb = [ps.enter_context(nc.sbuf_tensor(f"as_P{j}", [128, 2 * FC], BF16)) for j in range(2)]
        den = ps.enter_context(nc.sbuf_tensor("as_dn", [16, FC], F32))
        rcp = ps.enter_context(nc.sbuf_tensor("as_rc", [16, FC], F32))
        rct = [ps.enter_context(nc.sbuf_tensor(f"as_rt{j}", [128, 16], F32)) for j in range(4)]
        psc = [ps.enter_context(nc.psum_tensor(f"as_ps{j}", [128, FC], F32)) for j in range(2)]
        pdn = ps.enter_context(nc.psum_tensor("as_pd", [128, FC], F32))
        prt = ps.enter_context(nc.psum_tensor("as_pt", [128, FC], F32))
        with nc.Block() as blk:
            @blk.gpsimd
            def _(g):
                g.dma_start(wsc[:], ctx.ap("Wsc")[:]).then_inc(s.h["wl"], 16)
                g.dma_start(wdn[:], ctx.ap("Wdn")[:]).then_inc(s.h["wl"], 16)
                for ch in range(NFC):
                    for kt in range(2):
                        g.wait_ge(s.h["sc"], sc0 + PSC * ch + 2 * (kt + 1))
                        g.dma_start(ctx.ap("Pf").rearrange("(k p) t -> p k t", p=128)[:, kt, ch * FC:(ch + 1) * FC],
                                    Psb[ch % 2][:, kt * FC:(kt + 1) * FC]).then_inc(s.h["st"], 16)
                    for q_ in range(4):
                        g.wait_ge(s.h["sc"], sc0 + PSC * ch + 5 + q_ + 1)
                        g.dma_start(ctx.ap("rcptokD")[ch * FC + q_ * 128:ch * FC + (q_ + 1) * 128, :],
                                    rct[q_][:]).then_inc(s.h["st"], 16)
                g.wait_ge(s.h["st"], st0 + 16 * NFC * PST)
            s.add("wl", 16 * 2)
            s.add("st", 16 * NFC * PST)

            @blk.sync
            def _(sy):
                sy.dma_start(idf[:], ctx.ap("iden")[:]).then_inc(s.h["ld"], 16)
                for ch in range(NFC):
                    if ch >= 2:
                        sy.wait_ge(s.h["vc"], vc0 + PVC * (ch - 1))
                    sy.dma_start(qsb[ch % 2][:].rearrange("p (k c) -> p k c", c=FC),
                                 frearr(ctx.ap("QT"))[:, :, ch * FC:(ch + 1) * FC]).then_inc(s.h["ld"], 16)
                    for dup in range(2):
                        sy.dma_start(bass.AP(k2[ch % 2], dup * 64 * (16 * FC), [[16 * FC, 64], [FC, 16], [1, FC]]),
                                     bass.AP(ctx.ap("KTr"), ch * FC,
                                             [[T, 64], [64 * T, 16], [1, FC]])).then_inc(s.h["ld"], 16)
            s.add("ld", 16 * (1 + 3 * NFC))

            @blk.vector
            def _(v):
                for ch in range(NFC):
                    v.wait_ge(s.h["ld"], ld0 + 16 * (1 + 3 * (ch + 1)))
                    for tg in range(4):
                        gg = 4 * ch + tg
                        if gg >= 2:
                            v.wait_ge(s.h["tc"], tgdone(gg - 2))
                        base = (tg % 2) * 32
                        for t_ in range(4):
                            for hp in range(8):
                                j = t_ * 8 + hp
                                ins = v.tensor_mul(prod[:, (base + j) * FC:(base + j + 1) * FC],
                                                   qsb[ch % 2][:, hp * FC:(hp + 1) * FC],
                                                   k2[ch % 2][:, (tg * 4 + t_) * FC:(tg * 4 + t_ + 1) * FC])
                        ins.then_inc(s.h["vc"], 1)
                    v.wait_ge(s.h["sc"], sc0 + PSC * ch + 5)
                    if ch >= 1:
                        v.wait_ge(s.h["tc"], tc0 + PTC * (ch - 1) + 10)
                    v.reciprocal(rcp[:], den[:]).then_inc(s.h["vc"], 2)
            s.add("vc", PVC * NFC)

            @blk.tensor
            def _(tn):
                tn.wait_ge(s.h["wl"], wl0 + 32)
                tn.wait_ge(s.h["ld"], ld0 + 16)
                for ch in range(NFC):
                    for tg in range(4):
                        gg = 4 * ch + tg
                        tn.wait_ge(s.h["vc"], vc0 + PVC * ch + tg + 1)
                        if gg >= 2:
                            pg = gg - 2
                            tn.wait_ge(s.h["sc"], sc0 + PSC * (pg // 4) + (pg % 4) + 1)
                        base = (tg % 2) * 32
                        pb_ = 64 * (tg % 2)
                        for j in range(32):
                            ins = tn.matmul(psc[tg % 2][pb_:pb_ + 64, :], wsc[:, j * 64:(j + 1) * 64],
                                            prod[:, (base + j) * FC:(base + j + 1) * FC],
                                            start=(j == 0), stop=(j == 31))
                        ins.then_inc(s.h["tc"], 1)
                    tn.wait_ge(s.h["sc"], sc0 + PSC * ch + 4)
                    for kt in range(2):
                        tn.matmul(pdn[0:16, :], wdn[:], Psb[ch % 2][:, kt * FC:(kt + 1) * FC],
                                  start=(kt == 0), stop=(kt == 1)).then_inc(s.h["tc"], 1)
                    tn.wait_ge(s.h["vc"], vc0 + PVC * ch + 6)
                    for q_ in range(4):
                        if q_ >= 1:
                            tn.wait_ge(s.h["sc"], sc0 + PSC * ch + 5 + q_)
                        elif ch >= 1:
                            tn.wait_ge(s.h["sc"], sc0 + PSC * (ch - 1) + 9)
                        tn.transpose(prt[0:128, 0:16], rcp[:, q_ * 128:(q_ + 1) * 128], idf[0:16, 0:16]).then_inc(s.h["tc"], 1)
            s.add("tc", NFC * PTC)

            @blk.scalar
            def _(sc):
                for ch in range(NFC):
                    for tg in range(4):
                        sc.wait_ge(s.h["tc"], tgdone(4 * ch + tg))
                        if tg == 0 and ch >= 2:
                            sc.wait_ge(s.h["st"], st0 + 16 * PST * (ch - 1))
                        pb_ = 64 * (tg % 2)
                        sc.activation(Psb[ch % 2][pb_:pb_ + 64, (tg // 2) * FC:(tg // 2) * FC + FC],
                                      psc[tg % 2][pb_:pb_ + 64, :], AF.Exp,
                                      scale=ctx.cpool[pb_:pb_ + 64, CONST_IDX["attn_scale"]:CONST_IDX["attn_scale"] + 1]).then_inc(s.h["sc"], 1)
                    sc.wait_ge(s.h["tc"], tc0 + PTC * ch + 6)
                    if ch >= 1:
                        sc.wait_ge(s.h["vc"], vc0 + PVC * (ch - 1) + 6)
                    sc.copy(den[:], pdn[0:16, :]).then_inc(s.h["sc"], 1)
                    for q_ in range(4):
                        sc.wait_ge(s.h["tc"], tc0 + PTC * ch + 6 + q_ + 1)
                        if ch >= 1:
                            sc.wait_ge(s.h["st"], st0 + 16 * (PST * (ch - 1) + 2 + q_ + 1))
                        sc.copy(rct[q_][:], prt[0:128, 0:16]).then_inc(s.h["sc"], 1)
            s.add("sc", NFC * PSC)
    barrier(ctx)


def emit_pv(ctx):
    nc = ctx.nc
    s = ctx.sems
    TT_ = T // 128
    ld0, st0, vc0 = (s.cur(x) for x in ("ld", "st", "vc"))
    PVC = 3
    with contextlib.ExitStack() as ps:
        pb = [ps.enter_context(nc.sbuf_tensor(f"pvx_p{j}", [128, 256], BF16)) for j in range(2)]
        vb = [ps.enter_context(nc.sbuf_tensor(f"pvx_v{j}", [128, D], BF16)) for j in range(2)]
        rb = [ps.enter_context(nc.sbuf_tensor(f"pvx_r{j}", [128, 16], F32)) for j in range(2)]
        tmp = ps.enter_context(nc.sbuf_tensor("pvx_t", [128, D], F32))
        acc = ps.enter_context(nc.sbuf_tensor("pvx_a", [128, D], F32))
        ob = [ps.enter_context(nc.sbuf_tensor(f"pvx_o{j}", [128, D], BF16)) for j in range(2)]
        with nc.Block() as blk:
            @blk.sync
            def _(sy):
                for t in range(TT_):
                    if t >= 2:
                        sy.wait_ge(s.h["vc"], vc0 + PVC * (t - 1))
                    sy.dma_start(pb[t % 2][:], ctx.ap("Ptok")[t * 128:(t + 1) * 128, :]).then_inc(s.h["ld"], 16)
                    sy.dma_start(vb[t % 2][:], ctx.ap("Vtok")[t * 128:(t + 1) * 128, :]).then_inc(s.h["ld"], 16)
                    sy.dma_start(rb[t % 2][:], ctx.ap("rcptokD")[t * 128:(t + 1) * 128, :]).then_inc(s.h["ld"], 16)
            s.add("ld", 16 * 3 * TT_)

            @blk.vector
            def _(v):
                for t in range(TT_):
                    v.wait_ge(s.h["ld"], ld0 + 16 * 3 * (t + 1))
                    for th in range(16):
                        pap = bass.AP(pb[t % 2], th * 16, [[256, 128], [1, 16], [0, 64]])
                        vap = bass.AP(vb[t % 2], th * 64, [[D, 128], [0, 16], [1, 64]])
                        if th == 0:
                            v.tensor_tensor(acc[:], pap, vap, ALU.mult)
                        else:
                            v.tensor_tensor(tmp[:], pap, vap, ALU.mult)
                            ins = v.tensor_add(acc[:], acc[:], tmp[:])
                    ins.then_inc(s.h["vc"], 1)
                    rap = bass.AP(rb[t % 2], 0, [[16, 128], [1, 16], [0, 64]])
                    if t >= 2:
                        v.wait_ge(s.h["st"], st0 + 16 * (t - 1))
                    v.tensor_tensor(ob[t % 2][:], acc[:], rap, ALU.mult).then_inc(s.h["vc"], 2)
            s.add("vc", PVC * TT_)

            @blk.gpsimd
            def _(g):
                for t in range(TT_):
                    g.wait_ge(s.h["vc"], vc0 + PVC * (t + 1))
                    g.dma_start(ctx.ap("Ytok")[t * 128:(t + 1) * 128, :], ob[t % 2][:]).then_inc(s.h["st"], 16)
                g.wait_ge(s.h["st"], st0 + 16 * TT_)
            s.add("st", 16 * TT_)
    barrier(ctx)


def reg_l1_phases():
    phase("qx", ["x0tok"], ["xqtok"])(
        lambda ctx: emit_quant_phase(ctx, "qx", "x0tok", "xqtok", T, D, ln_first=True))
    phase("qwg", ["wgate"], ["wgq"])(
        lambda ctx: emit_quant_phase(ctx, "qwg", "wgate", "wgq", IF, D))
    phase("qwu", ["wup"], ["wuq"])(
        lambda ctx: emit_quant_phase(ctx, "qwu", "wup", "wuq", 2816, D))
    phase("qwd", ["wdown"], ["wdq"])(
        lambda ctx: emit_quant_phase(ctx, "qwd", "wdown", "wdq", D, IF))
    phase("xqTp", ["xqtok", "iden"], ["xqT"])(
        lambda ctx: emit_transpose_phase(ctx, "xqTp", "xqtok", "xqT", T, D))
    phase("wgTp", ["wgq", "iden"], ["wgqT"])(
        lambda ctx: emit_transpose_phase(ctx, "wgTp", "wgq", "wgqT", IF, D))
    phase("wuTp", ["wuq", "iden"], ["wuqT"])(
        lambda ctx: emit_transpose_phase(ctx, "wuTp", "wuq", "wuqT", IF, D))
    phase("wdTp", ["wdq", "iden"], ["wdqT"])(
        lambda ctx: emit_transpose_phase(ctx, "wdTp", "wdq", "wdqT", D, IF))
    phase("gateup", ["xqT", "wgqT", "wuqT"], ["hqtok"])(lambda ctx: emit_gateup(ctx))
    phase("hqTp", ["hqtok", "iden"], ["hqT"])(
        lambda ctx: emit_transpose_phase(ctx, "hqTp", "hqtok", "hqT", T, IF))
    phase("down", ["wdqT", "hqT"], ["g0T"])(
        lambda ctx: emit_matmul(ctx, "dn", "wdqT", IF, D, "hqT",
                                [dict(lo=0, hi=D, dst="g0T", dst_lo=0, kind="silu_f32")]))
    phase("x1", ["x0T", "sT", "g0T"], ["x1T"])(
        lambda ctx: emit_resmul(ctx, "x1", "x0T", "sT", "g0T", "x1T"))
    phase("ln1", ["x1T", "onescol", "ones1x128"], ["xn1bf", "xn1T"])(
        lambda ctx: emit_ln_feature(ctx, "x1T", [("xn1bf", True), ("xn1T", False)], tag="ln1"))
    phase("projq", ["WqT", "xn1bf"], ["QT"])(
        lambda ctx: emit_matmul(ctx, "pq", "WqT", D, D, "xn1bf",
                                [dict(lo=0, hi=D, dst="QT", dst_lo=0, kind="copy_bf16")]))
    phase("projk", ["WkT", "xn1bf"], ["KTr"])(
        lambda ctx: emit_matmul(ctx, "pk", "WkT", D, D, "xn1bf",
                                [dict(lo=0, hi=D, dst="KTr", dst_lo=0, kind="copy_bf16")]))
    phase("projv", ["WvT", "xn1bf"], ["Vtok"])(
        lambda ctx: emit_actlhs(ctx, "pv", "xn1bf", "WvT", D, "Vtok", out_bf16=True))
    phase("attn2a", ["QT", "KTr", "Wsc", "Wdn", "iden"], ["Pf", "rcptokD"])(
        lambda ctx: emit_attn_scores(ctx))
    phase("ptok", ["Pf", "iden"], ["Ptok"])(
        lambda ctx: emit_transpose_phase(ctx, "ptok", "Pf", "Ptok", 256, T))
    phase("pv", ["Ptok", "Vtok", "rcptokD"], ["Ytok"])(lambda ctx: emit_pv(ctx))
    phase("ytr", ["Ytok", "iden"], ["YT"])(
        lambda ctx: emit_transpose_phase(ctx, "ytr", "Ytok", "YT", T, D))
    phase("attnout", ["WfcT", "YT"], ["s1T"])(
        lambda ctx: emit_matmul(ctx, "ao", "WfcT", D, D, "YT",
                                [dict(lo=0, hi=D, dst="s1T", dst_lo=0, kind="copy_f32")]))
    phase("logits", ["xn1T", "routerw"], ["lgtok"])(
        lambda ctx: emit_actlhs(ctx, "lgt", "xn1T", "routerw", NE, "lgtok", out_bf16=False, fp32=True))
    phase("gates", ["lgtok", "routerb", "ones1x128", "iden"], ["gatesT"])(lambda ctx: emit_gates(ctx))
    phase("moe", ["gatesT", "xn1bf", "rw1", "rw3", "rw2", "sw1", "sw3", "sw2", "ones1x128"], ["g1T"])(
        lambda ctx: emit_moe(ctx))
    phase("outp", ["x1T", "s1T", "g1T"], ["outT"])(
        lambda ctx: emit_resmul(ctx, "outp", "x1T", "s1T", "g1T", "outT"))


reg_l1_phases()


def build_program(phase_names, roles):
    nc = bass.Bass("TRN2", target_bir_lowering=False)
    es = contextlib.ExitStack()
    tensors = {}
    for name, (shape, dt, _) in DRAM_SPEC.items():
        if name in roles:
            tensors[name] = nc.dram_tensor(name, shape, dt, kind=roles[name])
    ctx = Ctx(nc, es, tensors)
    ctx.sems = Sems(nc, es)
    with es:
        cpool_sb = es.enter_context(nc.sbuf_tensor("cpool_sb", [128, len(CONSTS)], F32))
        ctx.cpool = cpool_sb
        s = ctx.sems
        with nc.Block() as blk:
            @blk.sync
            def _(sy):
                sy.dma_start(cpool_sb[:], tensors["cpool"][:]).then_inc(s.get("ld"), 16)
            @blk.sync
            def _(sy):
                sy.wait_ge(s.h["ld"], 16)
        s.add("ld", 16)
        barrier(ctx)
        for pn in phase_names:
            PHASES[pn](ctx)
    return nc


def _t(a):
    return np.ascontiguousarray(np.asarray(a).T, dtype=np.float32)


def _c(a):
    return np.ascontiguousarray(np.asarray(a), dtype=np.float32)


def prep_shared(inp):
    w = {}
    for i in range(NL):
        w[f"inwT{i}"] = _t(inp["L0_attn_in_proj_w"][i])
        w[f"convw{i}"] = _c(inp["L0_attn_conv_w"][i])
        w[f"convb{i}"] = _c(inp["L0_attn_conv_b"][i])
        w[f"xpwT{i}"] = _t(inp["L0_attn_x_proj_w"][i])
        w[f"dtpwT{i}"] = _t(inp["L0_attn_dt_proj_w"][i])
        w[f"dtpb{i}"] = _c(inp["L0_attn_dt_proj_b"][i])
        w[f"Alog{i}"] = _c(inp["L0_attn_A_log"][i])
        w[f"Dp{i}"] = _c(inp["L0_attn_D_param"][i])
        w[f"outwT{i}"] = _t(inp["L0_attn_out_proj_w"][i])
        w[f"nw{i}"] = _c(inp["L0_attn_norm_w"][i])
        w[f"nb{i}"] = _c(inp["L0_attn_norm_b"][i])
    w["fnw"] = _c(inp["L0_attn_fnorm_w"])
    w["fnb"] = _c(inp["L0_attn_fnorm_b"])
    w["wgate"] = _c(inp["L0_ffn_w_gate"])
    w["wup"] = _c(inp["L0_ffn_w_up"])
    w["wdown"] = _c(inp["L0_ffn_w_down"])
    w["WqT"] = _t(inp["L1_attn_Wq"])
    w["WkT"] = _t(inp["L1_attn_Wk"])
    w["WvT"] = _t(inp["L1_attn_Wv"])
    w["WfcT"] = _t(inp["L1_attn_Wfc"])
    w["routerw"] = _c(inp["L1_ffn_router_w"])
    w["routerb"] = _c(inp["L1_ffn_router_b"])
    w["rw1"] = _c(inp["L1_ffn_routed_w1"])
    w["rw3"] = _c(inp["L1_ffn_routed_w3"])
    w["rw2"] = _c(inp["L1_ffn_routed_w2"])
    w["sw1"] = _c(inp["L1_ffn_shared_w1"])
    w["sw3"] = _c(inp["L1_ffn_shared_w3"])
    w["sw2"] = _c(inp["L1_ffn_shared_w2"])
    w["cpool"] = np.tile(np.array(list(CONSTS.values()), np.float32)[None, :], (128, 1))
    w["iden"] = np.eye(128, dtype=np.float32)
    w["ones1x128"] = np.ones((1, 128), dtype=np.float32)
    w["onescol"] = np.ones((128, 1), dtype=np.float32)
    es = np.zeros((32, 32, 128), dtype=np.float32)
    for n in range(32):
        es[n, n, :] = 1.0
    w["Esel"] = es.reshape(32, 32 * 128)
    wsc = np.zeros((128, 32, 64), dtype=np.float32)
    for ts in range(4):
        for hp in range(8):
            j = ts * 8 + hp
            for half in range(2):
                wsc[half * 64:(half + 1) * 64, j, ts * 16 + 2 * hp + half] = 1.0
    w["Wsc"] = np.ascontiguousarray(wsc.reshape(128, 32 * 64))
    wdn = np.zeros((128, 16), dtype=np.float32)
    for t_ in range(8):
        for h_ in range(16):
            wdn[t_ * 16 + h_, h_] = 1.0
    w["Wdn"] = wdn
    return w


def prep_core(inp, c):
    x = np.asarray(inp["L0_attn_hidden_states"])
    xc_ = x[BPC * c:BPC * (c + 1)].reshape(T, D)
    return {"x0T": _t(xc_), "x0tok": _c(xc_)}


FULL_CHAIN = ["ln0", "lnm0"]
for _i in range(NL):
    FULL_CHAIN += [f"inproj{_i}", f"conv{_i}", f"xproj{_i}", f"dtproj{_i}",
                   f"scan{_i}", f"outproj{_i}"]
    FULL_CHAIN += [f"lnm{_i + 1}"] if _i < NL - 1 else ["lnfin"]
FULL_CHAIN += ["qx", "qwg", "qwu", "qwd", "xqTp", "wgTp", "wuTp", "wdTp",
               "gateup", "hqTp", "down", "x1", "ln1",
               "projq", "projk", "projv", "attn2a", "ptok", "pv", "ytr", "attnout",
               "logits", "gates", "moe", "outp"]

LAUNCH_GROUPS = [FULL_CHAIN]


def run_launches(groups, shared, percore_list, want):
    env = [dict(shared, **percore_list[c]) for c in range(len(percore_list))]
    ncores = len(percore_list)
    for gi, group in enumerate(groups):
        ins, outs = set(), set()
        for pn in group:
            pi, po = PHASE_IO[pn]
            ins |= {n for n in pi if n not in outs}
            outs |= set(po)
        later = set(want)
        for g2 in groups[gi + 1:]:
            for pn in g2:
                later |= set(PHASE_IO[pn][0])
        ins |= {"cpool"}
        roles = {n: "ExternalInput" for n in ins}
        for n in outs:
            roles[n] = "ExternalOutput" if n in later else "Internal"
        nc = build_program(group, roles)
        in_maps = [{n: env[c][n] for n in ins} for c in range(ncores)]
        res = run_bass_kernel_spmd(nc, in_maps, core_ids=list(range(ncores)))
        for c in range(ncores):
            for n, arr in res.results[c].items():
                env[c][n] = np.asarray(arr)
    return env


def kernel(**inputs):
    shared = prep_shared(inputs)
    percore = [prep_core(inputs, c) for c in range(NCORES)]
    env = run_launches(LAUNCH_GROUPS, shared, percore, ["outT"])
    outs = []
    for c in range(NCORES):
        outs.append(np.asarray(env[c]["outT"], dtype=np.float32).T.reshape(BPC, L, D))
    return np.concatenate(outs, axis=0).astype(np.float32)
